# version 11
# baseline (speedup 1.0000x reference)
import contextlib
import numpy as np

import concourse.bass as bass
import concourse.mybir as mybir
from concourse.bass_utils import run_bass_kernel_spmd

F32 = mybir.dt.float32
BF16 = mybir.dt.bfloat16
I32 = mybir.dt.int32
AF_ = mybir.ActivationFunctionType
ALU = mybir.AluOpType

D = 2048
I5 = 5504
KD = 16
KI = 43
TOK = 2048
G = 512
NST = 2
IPADC = 48
BCOL = 768
UCOL = 256
GSZ = 128 * KD * BCOL
DSZ = 128 * KI * 256
SHSZ = 2 * GSZ + DSZ
EPS = 1e-5
INVD = 1.0 / D


class Prog:
    ENGS = ("sync", "gpsimd", "vector", "scalar", "tensor")

    def __init__(self):
        self.q = {e: [] for e in self.ENGS}
        self.sems = {}

    def new_sem(self, nc, es, *names):
        for n in names:
            h = es.enter_context(nc.semaphore(n))
            self.sems[n] = [h, 0]

    def n(self, name):
        return self.sems[name][1]

    def emit(self, eng, fn):
        self.q[eng].append(fn)

    def wait(self, eng, name, val=None):
        v = self.sems[name][1] if val is None else val
        if v <= 0:
            return
        h = self.sems[name][0]
        self.emit(eng, lambda e, h=h, v=v: e.wait_ge(h, v))

    def op(self, eng, build, inc=None, k=1, inc2=None, k2=1):
        h = None
        if inc is not None:
            self.sems[inc][1] += k
            h = self.sems[inc][0]
        h2 = None
        if inc2 is not None:
            self.sems[inc2][1] += k2
            h2 = self.sems[inc2][0]

        def f(e, build=build, h=h, k=k, h2=h2, k2=k2):
            ins = build(e)
            if h is not None:
                ins = ins.then_inc(h, k)
            if h2 is not None:
                ins.then_inc(h2, k2)
        self.emit(eng, f)

    def flush(self, block):
        q = self.q

        @block.sync
        def _(e):
            for f in q["sync"]:
                f(e)

        @block.gpsimd
        def _(e):
            for f in q["gpsimd"]:
                f(e)

        @block.vector
        def _(e):
            for f in q["vector"]:
                f(e)

        @block.scalar
        def _(e):
            for f in q["scalar"]:
                f(e)

        @block.tensor
        def _(e):
            for f in q["tensor"]:
                f(e)
        self.q = {e: [] for e in self.ENGS}


def build_nc():
    nc = bass.Bass("TRN2", target_bir_lowering=False, detect_race_conditions=False)

    xin = nc.dram_tensor("xin", [D, TOK], F32, kind="ExternalInput").ap()
    waT = nc.dram_tensor("waT", [D, D], F32, kind="ExternalInput").ap()
    wxT = nc.dram_tensor("wxT", [D, D], F32, kind="ExternalInput").ap()
    bafm = nc.dram_tensor("bafm", [128, KD], F32, kind="ExternalInput").ap()
    bxfm = nc.dram_tensor("bxfm", [128, KD], F32, kind="ExternalInput").ap()
    lamfm = nc.dram_tensor("lamfm", [128, KD], F32, kind="ExternalInput").ap()
    lutin = nc.dram_tensor("lutin", [1, 16], F32, kind="ExternalInput").ap()
    selin = nc.dram_tensor("selin", [1, 8], F32, kind="ExternalInput").ap()
    gsh = nc.dram_tensor("gsh", [D, BCOL], I32, kind="ExternalInput").ap()
    ush = nc.dram_tensor("ush", [D, BCOL], I32, kind="ExternalInput").ap()
    dsh = nc.dram_tensor("dsh", [I5, 256], I32, kind="ExternalInput").ap()
    outfm = nc.dram_tensor("outfm", [D, TOK], F32, kind="ExternalOutput").ap()

    wsh_ga = nc.dram_tensor("wsh_ga", [1, GSZ // 3], BF16, kind="Internal").ap()
    wfl_ga = nc.dram_tensor("wfl_ga", [8, GSZ // 3], BF16, kind="Internal").ap()
    wsh_gb = nc.dram_tensor("wsh_gb", [1, 2 * GSZ // 3], BF16, kind="Internal").ap()
    wfl_gb = nc.dram_tensor("wfl_gb", [8, 2 * GSZ // 3], BF16, kind="Internal").ap()
    wsh_u = nc.dram_tensor("wsh_u", [1, GSZ], BF16, kind="Internal").ap()
    wfl_u = nc.dram_tensor("wfl_u", [8, GSZ], BF16, kind="Internal").ap()
    wsh_d = nc.dram_tensor("wsh_d", [1, DSZ], BF16, kind="Internal").ap()
    wfl_d = nc.dram_tensor("wfl_d", [8, DSZ], BF16, kind="Internal").ap()
    wabf = nc.dram_tensor("wabf", [128, 8 * KD * 256], BF16, kind="Internal").ap()
    wxbf = nc.dram_tensor("wxbf", [128, 8 * KD * 256], BF16, kind="Internal").ap()
    x1d = nc.dram_tensor("x1d", [KD, 128, TOK], F32, kind="Internal").ap()
    x2d = nc.dram_tensor("x2d", [KD, 128, TOK], F32, kind="Internal").ap()
    cd = nc.dram_tensor("cd", [KD, 128, TOK], BF16, kind="Internal").ap()
    cryl = nc.dram_tensor("cryl", [1, 128 * KD], F32, kind="Internal").ap()
    crya = nc.dram_tensor("crya", [2, 128 * KD], F32, kind="Internal").ap()

    def gview_a(r):
        return wfl_ga[r, 0:GSZ // 3].rearrange("(p j c) -> p j c", p=128, j=KD)

    def gview_b(r):
        return wfl_gb[r, 0:2 * GSZ // 3].rearrange(
            "(p u j c) -> p u j c", p=128, u=2, j=KD)

    def uview(r):
        return wfl_u[r, 0:GSZ].rearrange("(p u j c) -> p u j c", p=128, u=3, j=KD)

    def dview(r):
        return wfl_d[r, 0:DSZ].rearrange("(p h i c) -> p h i c", p=128, h=2, i=KI)

    P = Prog()

    with contextlib.ExitStack() as es:
        sb = nc.sbuf_tensor
        lutsb = es.enter_context(sb("lutsb", [1, 16], F32))
        selsb = es.enter_context(sb("selsb", [1, 8], F32))
        ones1 = es.enter_context(sb("ones1", [1, 128], F32))
        onesb = es.enter_context(sb("onesb", [128, 128], BF16))
        kcon = es.enter_context(sb("kcon", [128, 16], F32))
        lutpp = es.enter_context(sb("lutpp", [128, 16], F32))
        selpp = es.enter_context(sb("selpp", [128, 8], F32))
        bapp = es.enter_context(sb("bapp", [128, KD], F32))
        bxpp = es.enter_context(sb("bxpp", [128, KD], F32))
        lampp = es.enter_context(sb("lampp", [128, KD], F32))
        kap = es.enter_context(sb("kap", [128, KD], F32))
        kap2 = es.enter_context(sb("kap2", [128, KD], F32))
        spt = es.enter_context(sb("spt", [128, KD], F32))
        epspp = es.enter_context(sb("epspp", [128, 1], F32))
        sps = es.enter_context(sb("sps", [128, KD], F32))
        hend = es.enter_context(sb("hend", [128, KD], F32))
        hin = es.enter_context(sb("hin", [128, KD], F32))
        hall = es.enter_context(sb("hall", [128, 2, KD], F32))
        AF = es.enter_context(sb("AF", [128, 20480], F32))
        AH = es.enter_context(sb("AH", [128, 44032], BF16))
        AW = es.enter_context(sb("AW", [128, 16384], BF16))
        PS = es.enter_context(nc.psum_tensor("PS", [128, 4096], F32))

        AFb = AF.ap().bitcast(BF16)

        P.new_sem(nc, es,
                  "prm", "vini", "pebc", "scbc", "scsp", "vkap",
                  "igg", "igu", "igu2", "igd", "wc", "ws", "cc",
                  "xld", "x1st", "x2st", "ost", "hld",
                  "wsA", "wsB",
                  "scq", "teq", "pest", "veln",
                  "sch", "veh", "ped", "ved",
                  "peri", "scri", "sca", "vet", "scsq", "veu", "vesc", "vehc",
                  "vecor")

        for dst, src in ((lutsb, lutin), (selsb, selin), (bapp, bafm),
                         (bxpp, bxfm), (lampp, lamfm)):
            P.op("sync", lambda e, d=dst, s=src: e.dma_start(out=d[:, :], in_=s[:, :]),
                 inc="prm", k=16)
        PRM_T = P.n("prm")

        for k in range(16):
            P.op("vector", lambda e, k=k: e.memset(kcon[:, k:k + 1], float(k)))
        P.op("vector", lambda e: e.memset(ones1[:, :], 1.0))
        P.op("vector", lambda e: e.memset(epspp[:, :], EPS))
        P.op("vector", lambda e: e.memset(onesb[:, :], 1.0), inc="vini")

        P.wait("tensor", "prm", PRM_T)
        P.wait("tensor", "vini", 1)
        P.op("tensor", lambda e: e.matmul(PS[:, 0:16], ones1[:, :], lutsb[:, :],
                                          start=True, stop=True), inc="pebc")
        P.op("tensor", lambda e: e.matmul(PS[:, 512:520], ones1[:, :], selsb[:, :],
                                          start=True, stop=True), inc="pebc")
        P.wait("scalar", "pebc", 2)
        P.op("scalar", lambda e: e.copy(lutpp[:, :], PS[:, 0:16]), inc="scbc")
        P.op("scalar", lambda e: e.copy(selpp[:, :], PS[:, 512:520]), inc="scbc")
        P.wait("scalar", "prm", PRM_T)
        P.op("scalar", lambda e: e.activation(out=spt[:, :], in_=lampp[:, :],
                                              func=AF_.Exp, scale=-1.0))
        P.op("scalar", lambda e: e.activation(out=sps[:, :], in_=spt[:, :],
                                              func=AF_.Ln, bias=1.0), inc="scsp")
        P.wait("vector", "scsp", 1)
        P.op("vector", lambda e: e.tensor_scalar(out=kap[:, :], in0=sps[:, :],
                                                 scalar1=-8.0, scalar2=None,
                                                 op0=ALU.mult))
        P.op("vector", lambda e: e.tensor_scalar(out=kap2[:, :], in0=sps[:, :],
                                                 scalar1=-16.0, scalar2=None,
                                                 op0=ALU.mult), inc="vkap")

        idx_g3 = AH[:, 0:KD * BCOL].rearrange("p (u j c) -> p u j c", u=3, j=KD)
        P.op("gpsimd", lambda e: e.dma_start(
            out=idx_g3, in_=gsh.rearrange("(j p) (u c) -> p u j c", p=128, u=3)),
            inc="igg", k=16)

        dec_g = AFb[:, 16384:16384 + KD * BCOL]
        msk = AW[:, 0:KD * BCOL]
        marks = {}

        def emit_decode(idx_ap, dec_ap, nfree, msk_ap, inc_last="vesc"):
            for k in range(16):
                if k == 0:
                    P.op("vector", lambda e, i=idx_ap, d=dec_ap: e.tensor_scalar(
                        out=d, in0=i, scalar1=kcon[:, 0:1], scalar2=lutpp[:, 0:1],
                        op0=ALU.is_equal, op1=ALU.mult))
                else:
                    P.op("vector", lambda e, i=idx_ap, k=k, n=nfree, m=msk_ap:
                         e.tensor_scalar(
                             out=m[:, 0:n], in0=i, scalar1=kcon[:, k:k + 1],
                             scalar2=lutpp[:, k:k + 1], op0=ALU.is_equal, op1=ALU.mult))
                    P.op("vector", lambda e, d=dec_ap, n=nfree, m=msk_ap:
                         e.tensor_tensor(out=d, in0=d, in1=m[:, 0:n], op=ALU.add),
                         inc=(inc_last if k == 15 else None))

        P.wait("vector", "scbc", 2)
        P.wait("vector", "igg", 16)
        TUG = KD * 256
        emit_decode(AH[:, 0:TUG], dec_g[:, 0:TUG], TUG, msk)
        P.wait("gpsimd", "vesc", P.n("vesc"))
        P.op("gpsimd", lambda e: e.dma_start(
            out=wsh_ga[0, :].rearrange("(p x) -> p x", p=128),
            in_=dec_g[:, 0:TUG]), inc="ws", k=16)
        P.wait("gpsimd", "ws", P.n("ws"))
        P.op("gpsimd", lambda e: e.collective_compute(
            "AllGather", ALU.bypass, replica_groups=[list(range(8))],
            ins=[wsh_ga.opt()], outs=[wfl_ga.opt()]), inc="cc")
        for uu in (1, 2):
            emit_decode(AH[:, uu * TUG:(uu + 1) * TUG],
                        dec_g[:, uu * TUG:(uu + 1) * TUG], TUG, msk)
            P.wait("gpsimd", "vesc", P.n("vesc"))
            P.op("gpsimd", lambda e, uu=uu: e.dma_start(
                out=wsh_gb[0, :].rearrange("(p x) -> p x", p=128)[
                    :, (uu - 1) * TUG:uu * TUG],
                in_=dec_g[:, uu * TUG:(uu + 1) * TUG]), inc="ws", k=16)
        P.wait("gpsimd", "ws", P.n("ws"))
        P.op("gpsimd", lambda e: e.collective_compute(
            "AllGather", ALU.bypass, replica_groups=[list(range(8))],
            ins=[wsh_gb.opt()], outs=[wfl_gb.opt()]), inc="cc")
        marks["ws_g"] = P.n("ws")

        wshd_pm = wsh_d[0, 0:DSZ].rearrange("(p x) -> p x", p=128)
        wshu_pm = wsh_u[0, 0:GSZ].rearrange("(p x) -> p x", p=128)
        QD = 2752
        HU = KD * BCOL // 2

        def emit_down_up_decode_hook():
            TU = KD * 256
            uview_in = ush.rearrange("(j p) (u c) -> p u j c", p=128, u=3)
            t_vesc = {}
            for tt in range(3):
                aw_sl = 8192 + (tt % 2) * TU
                dec_sl = (tt % 2) * TU
                usem = "igu" if tt % 2 == 0 else "igu2"
                if tt >= 2:
                    P.wait("gpsimd", "vesc", t_vesc[tt - 2])
                P.op("gpsimd", lambda e, sl=aw_sl, tt=tt: e.dma_start(
                    out=AW[:, sl:sl + TU], in_=uview_in[:, tt]), inc=usem, k=16)
                idx_t = AW[:, aw_sl:aw_sl + TU]
                dec_t = AFb[:, dec_sl:dec_sl + TU]
                msk_t = AFb[:, 2 * TU:3 * TU]
                P.wait("vector", usem, P.n(usem))
                if tt >= 2:
                    P.wait("vector", "ws", P.n("ws"))
                emit_decode(idx_t, dec_t, TU, msk_t)
                t_vesc[tt] = P.n("vesc")
                P.wait("gpsimd", "vesc", P.n("vesc"))
                P.op("gpsimd", lambda e, tt=tt, sl=dec_sl: e.dma_start(
                    out=wshu_pm[:, tt * TU:(tt + 1) * TU],
                    in_=AFb[:, sl:sl + TU]), inc="ws", k=16)
            P.wait("gpsimd", "ws", P.n("ws"))
            P.op("gpsimd", lambda e: e.collective_compute(
                "AllGather", ALU.bypass, replica_groups=[list(range(8))],
                ins=[wsh_u.opt()], outs=[wfl_u.opt()]), inc="cc")
            HD = KI * 128
            dview_in = dsh.rearrange("(i p) (h c) -> p h i c", p=128, h=2)
            for hh in range(2):
                P.wait("gpsimd", "vesc", P.n("vesc"))
                P.op("gpsimd", lambda e, hh=hh: e.dma_start(
                    out=AW[:, 8192:8192 + HD], in_=dview_in[:, hh]),
                    inc="igd", k=16)
                P.wait("vector", "igd", P.n("igd"))
                P.wait("vector", "ws", P.n("ws"))
                emit_decode(AW[:, 8192:8192 + HD], AFb[:, 0:HD], HD,
                            AFb[:, HD:2 * HD])
                P.wait("gpsimd", "vesc", P.n("vesc"))
                P.op("gpsimd", lambda e, hh=hh: e.dma_start(
                    out=wshd_pm[:, hh * HD:(hh + 1) * HD],
                    in_=AFb[:, 0:HD]), inc="ws", k=16)
            P.wait("gpsimd", "ws", P.n("ws"))
            P.op("gpsimd", lambda e: e.collective_compute(
                "AllGather", ALU.bypass, replica_groups=[list(range(8))],
                ins=[wsh_d.opt()], outs=[wfl_d.opt()]), inc="cc")
            marks["ws_dec"] = P.n("ws")
            P.wait("vector", "ws", P.n("ws"))
            for aa in range(8):
                P.op("gpsimd", lambda e, a=aa: e.dma_start(
                    out=wabf.rearrange("p (a j c) -> p a j c", a=8, j=KD)[:, a],
                    in_=waT.rearrange("(j p) (a c) -> p a j c", p=128, a=8)[:, a]),
                    inc="wc", k=16)
                P.op("gpsimd", lambda e, a=aa: e.dma_start(
                    out=wxbf.rearrange("p (a j c) -> p a j c", a=8, j=KD)[:, a],
                    in_=wxT.rearrange("(j p) (a c) -> p a j c", p=128, a=8)[:, a]),
                    inc="wc", k=16)
            marks["wc_t"] = P.n("wc")

        xg = AF[:, 0:8192].rearrange("p (j t) -> p j t", j=KD)
        lnx = AFb[:, 16384:32768].rearrange("p (j t) -> p j t", j=KD)
        stmp = AF[:, 16384:17408]
        mrep = AF[:, 17408:17920]
        msq = AF[:, 17920:18432]
        var = AF[:, 18432:18944]
        rstd = AF[:, 18944:19456]
        xbuf = AFb[:, 38912:40960].rearrange("p (b t) -> p b t", b=4)
        h_all = AH[:, 0:KI * 1024].rearrange("p (i t) -> p i t", i=KI)
        wg_sl = [AW[:, 0:4096].rearrange("p (j c) -> p j c", j=KD),
                 AW[:, 4096:8192].rearrange("p (j c) -> p j c", j=KD)]
        wu_sl = [AW[:, 8192:12288].rearrange("p (j c) -> p j c", j=KD),
                 AW[:, 12288:16384].rearrange("p (j c) -> p j c", j=KD)]
        wd_sl = [AW[:, 0:5504].rearrange("p (i c) -> p i c", i=KI),
                 AW[:, 5504:11008].rearrange("p (i c) -> p i c", i=KI)]

        def emit_correction(cgrp):
            for j in range(KD):
                P.op("vector", lambda e, j=j, c=cgrp: e.scalar_tensor_tensor(
                    out=xg[:, j, :], in0=c[:, j, :], scalar=hin[:, j:j + 1],
                    in1=xg[:, j, :], op0=ALU.mult, op1=ALU.add),
                    inc=("vecor" if j == KD - 1 else None))

        def emit_group_ln(g2):
            base_teq = P.n("teq")
            for j in range(KD):
                if j >= 2:
                    P.wait("scalar", "teq", base_teq + 2 * (j - 1))
                P.op("scalar", lambda e, j=j: e.copy(
                    xbuf[:, j % 2, :], xg[:, j, :]), inc="scq")
                P.op("scalar", lambda e, j=j: e.activation(
                    out=xbuf[:, 2 + j % 2, :], in_=xg[:, j, :],
                    func=AF_.Square), inc="scq")
            P.wait("tensor", "vet")
            base_scq = P.n("scq") - 2 * KD
            for j in range(KD):
                P.wait("tensor", "scq", base_scq + 2 * (j + 1))
                P.op("tensor", lambda e, j=j: e.matmul(
                    PS[:, 6 * 512:7 * 512], onesb[:, :], xbuf[:, j % 2, :],
                    start=(j == 0), stop=(j == KD - 1)), inc="teq")
                P.op("tensor", lambda e, j=j: e.matmul(
                    PS[:, 7 * 512:8 * 512], onesb[:, :], xbuf[:, 2 + j % 2, :],
                    start=(j == 0), stop=(j == KD - 1)),
                    inc=("pest" if j == KD - 1 else "teq"))
            P.wait("vector", "pest", P.n("pest"))
            P.op("vector", lambda e: e.tensor_scalar(
                out=mrep[:, :], in0=PS[:, 6 * 512:7 * 512], scalar1=INVD,
                scalar2=None, op0=ALU.mult))
            P.op("vector", lambda e: e.tensor_tensor(
                out=msq[:, :], in0=mrep[:, :], in1=mrep[:, :], op=ALU.mult))
            P.op("vector", lambda e: e.scalar_tensor_tensor(
                out=var[:, :], in0=PS[:, 7 * 512:8 * 512], scalar=INVD,
                in1=msq[:, :], op0=ALU.mult, op1=ALU.subtract), inc="vet")
            P.wait("scalar", "vet", P.n("vet"))
            P.op("scalar", lambda e: e.activation(
                out=msq[:, :], in_=var[:, :], func=AF_.Ln, bias=epspp[:, 0:1]))
            P.op("scalar", lambda e: e.activation(
                out=rstd[:, :], in_=msq[:, :], func=AF_.Exp, scale=-0.5), inc="sca")
            P.wait("vector", "sca", P.n("sca"))
            for j in range(KD):
                P.op("vector", lambda e, j=j: e.tensor_tensor(
                    out=stmp[:, 0:512], in0=xg[:, j, :], in1=mrep[:, :],
                    op=ALU.subtract))
                P.op("vector", lambda e, j=j, g2=g2: e.tensor_tensor(
                    out=lnx[:, j, g2 * G:(g2 + 1) * G], in0=stmp[:, 0:512],
                    in1=rstd[:, :], op=ALU.mult),
                    inc=("veln" if j == KD - 1 else None))

        def emit_ffn(pass_id, hook_after_ln=None):
            is3 = (pass_id == 3)
            st_sem = "ost" if is3 else "x1st"
            for st in range(NST):
                for g2 in range(2):
                    g = 2 * st + g2
                    gs = g * G
                    P.wait("sync", "veln")
                    P.wait("sync", st_sem, P.n(st_sem))
                    if is3:
                        P.wait("sync", "vecor")
                        P.wait("sync", "x2st", P.n("x2st"))
                        cgrp = lnx[:, :, g2 * G:(g2 + 1) * G]
                        P.op("sync", lambda e, gs=gs: e.dma_start(
                            out=xg[:, :, :],
                            in_=x2d.rearrange("j p t -> p j t")[:, :, gs:gs + G]),
                            inc="xld", k=16)
                        P.op("sync", lambda e, gs=gs, c=cgrp: e.dma_start(
                            out=c, in_=cd.rearrange("j p t -> p j t")[:, :, gs:gs + G]),
                            inc="xld", k=16)
                        P.wait("vector", "xld", P.n("xld"))
                        emit_correction(cgrp)
                        P.wait("scalar", "vecor", P.n("vecor"))
                    else:
                        P.op("sync", lambda e, gs=gs: e.dma_start(
                            out=xg[:, :, :],
                            in_=xin.rearrange("(j p) t -> p j t", p=128)[:, :, gs:gs + G]),
                            inc="xld", k=16)
                        P.wait("scalar", "xld", P.n("xld"))
                    if not is3 and st == 0:
                        P.wait("vector", "ws", marks["ws_g"])
                    emit_group_ln(g2)
                if hook_after_ln is not None and st == 0:
                    hook_after_ln()
                    hook_after_ln = None
                VLN = P.n("veln")

                P.wait("tensor", "veln", VLN)
                P.wait("tensor", "cc", 1)
                P.wait("tensor", "scbc", 2)
                PED0 = P.n("ped")
                PERI0 = P.n("peri")
                unit_ped = {}
                it_n = 0
                gate_units = [(uu, r) for uu in range(3) for r in range(8)
                              if 6 * r + 2 * uu <= 42]
                for u, (uu, r) in enumerate(gate_units):
                    b = u % 2
                    if u < 2:
                        P.wait("sync", "ped", PED0)
                        P.wait("sync", "peri", PERI0)
                    else:
                        P.wait("sync", "ped", unit_ped[u - 2])
                    if uu == 0 and r == 0:
                        P.wait("sync", "cc", 1)
                    if uu == 1 and r == 0:
                        P.wait("sync", "cc", 2)
                    sem = "wsA" if b == 0 else "wsB"
                    if uu == 0:
                        P.op("sync", lambda e, r=r, b=b: e.dma_start(
                            out=wg_sl[b][:, :, :], in_=gview_a(r)), inc=sem, k=16)
                    else:
                        P.op("sync", lambda e, r=r, uu=uu, b=b: e.dma_start(
                            out=wg_sl[b][:, :, :], in_=gview_b(r)[:, uu - 1]),
                            inc=sem, k=16)
                    val_u = P.n(sem)
                    for mi, m in enumerate((6 * r + 2 * uu, 6 * r + 2 * uu + 1)):
                        if m >= KI:
                            continue
                        col = mi * 128
                        pp = it_n % 2
                        it_n += 1
                        P.wait("tensor", sem, val_u)
                        if it_n > 2:
                            P.wait("tensor", "sch", P.n("sch") - 1)
                        for k in range(KD):
                            for g2 in range(2):
                                bk = 2 * pp + g2
                                last = (k == KD - 1 and g2 == 1)
                                P.op("tensor",
                                     lambda e, b=b, k=k, col=col, bk=bk,
                                     g2=g2: e.matmul(
                                         PS[:, bk * 512:(bk + 1) * 512],
                                         wg_sl[b][:, k, col:col + 128],
                                         lnx[:, k, g2 * G:(g2 + 1) * G],
                                         start=(k == 0), stop=(k == KD - 1)),
                                     inc=("ped" if last else None))
                        P.wait("scalar", "ped", P.n("ped"))
                        P.op("scalar", lambda e, m=m, pp=pp: e.activation(
                            out=h_all[:, m, :],
                            in_=PS[:, (2 * pp) * 512:(2 * pp + 2) * 512],
                            func=AF_.Silu), inc="sch")
                    unit_ped[u] = P.n("ped")

                P.wait("sync", "cc", 3 if not is3 else 0)
                unit_ped_u = {}
                for u in range(22):
                    b = u % 2
                    r, uu = u // 3, u % 3
                    if u < 2:
                        P.wait("sync", "ped", unit_ped[20 + u])
                        if u == 0 and not is3 and st == 0 and "ws_dec" in marks:
                            P.wait("sync", "ws", marks["ws_dec"])
                    else:
                        P.wait("sync", "ped", unit_ped_u[u - 2])
                    sem = "wsA" if b == 0 else "wsB"
                    P.op("sync", lambda e, r=r, uu=uu, b=b: e.dma_start(
                        out=wu_sl[b][:, :, :], in_=uview(r)[:, uu]), inc=sem, k=16)
                    val_u = P.n(sem)
                    for m in (2 * u, 2 * u + 1):
                        if m >= KI:
                            continue
                        col = (m % 2) * 128
                        pp = m % 2
                        P.wait("tensor", sem, val_u)
                        if m >= 2:
                            P.wait("tensor", "veh", P.n("veh") - 1)
                        elif u == 0:
                            P.wait("tensor", "sch", P.n("sch"))
                        for k in range(KD):
                            for g2 in range(2):
                                bk = 4 + 2 * pp + g2
                                last = (k == KD - 1 and g2 == 1)
                                P.op("tensor",
                                     lambda e, b=b, k=k, col=col, bk=bk,
                                     g2=g2: e.matmul(
                                         PS[:, bk * 512:(bk + 1) * 512],
                                         wu_sl[b][:, k, col:col + 128],
                                         lnx[:, k, g2 * G:(g2 + 1) * G],
                                         start=(k == 0), stop=(k == KD - 1)),
                                     inc=("ped" if last else None))
                        P.wait("vector", "ped", P.n("ped"))
                        P.op("vector", lambda e, m=m, pp=pp: e.tensor_tensor(
                            out=h_all[:, m, :], in0=h_all[:, m, :],
                            in1=PS[:, (4 + 2 * pp) * 512:(4 + 2 * pp + 2) * 512],
                            op=ALU.mult), inc="veh")
                    unit_ped_u[u] = P.n("ped")

                GU_VEH = P.n("veh")
                PED_DN = P.n("ped")
                round_ped = {}
                dn_round = 0
                for g2 in range(2):
                    g = 2 * st + g2
                    gs = g * G
                    P.wait("sync", "veln", VLN)
                    if not is3 and st == 0 and g2 == 0 and "ws_dec" in marks:
                        P.wait("sync", "ws", marks["ws_dec"])
                    if g2 == 1:
                        P.wait("sync", st_sem, P.n(st_sem))
                    if is3:
                        P.wait("sync", "vecor")
                        P.wait("sync", "ped", PED_DN)
                        cgrp = lnx[:, :, g2 * G:(g2 + 1) * G]
                        P.op("sync", lambda e, gs=gs: e.dma_start(
                            out=xg[:, :, :],
                            in_=x2d.rearrange("j p t -> p j t")[:, :, gs:gs + G]),
                            inc="xld", k=16)
                        P.op("sync", lambda e, gs=gs, c=cgrp: e.dma_start(
                            out=c, in_=cd.rearrange("j p t -> p j t")[:, :, gs:gs + G]),
                            inc="xld", k=16)
                        P.wait("vector", "xld", P.n("xld"))
                        emit_correction(cgrp)
                    else:
                        P.op("sync", lambda e, gs=gs: e.dma_start(
                            out=xg[:, :, :],
                            in_=xin.rearrange("(j p) t -> p j t", p=128)[:, :, gs:gs + G]),
                            inc="xld", k=16)
                        P.wait("vector", "xld", P.n("xld"))
                    for f in range(KD):
                        db = dn_round % 2
                        rb, ch = f // 2, f % 2
                        if dn_round < 2:
                            P.wait("sync", "ped", PED_DN)
                        else:
                            P.wait("sync", "ped", round_ped[dn_round - 2])
                        sem = "wsA" if db == 0 else "wsB"
                        if dn_round == 0 and st == 0 and pass_id == 1:
                            P.wait("sync", "cc", 4)
                        P.op("sync", lambda e, rb=rb, ch=ch, db=db: e.dma_start(
                            out=wd_sl[db][:, :, :],
                            in_=dview(rb)[:, ch]), inc=sem, k=16)
                        val_u = P.n(sem)
                        P.wait("tensor", sem, val_u)
                        if g2 == 0 and f == 0:
                            P.wait("tensor", "veh", GU_VEH)
                        slot = dn_round % 4
                        if dn_round >= 4:
                            P.wait("tensor", "ved", P.n("ved") - 3)
                        for i in range(KI):
                            P.op("tensor", lambda e, i=i, db=db, slot=slot,
                                 g2=g2: e.matmul(
                                     PS[:, slot * 512:(slot + 1) * 512],
                                     wd_sl[db][:, i, :],
                                     h_all[:, i, g2 * G:(g2 + 1) * G],
                                     start=(i == 0), stop=(i == KI - 1)),
                                 inc=("ped" if i == KI - 1 else None))
                        round_ped[dn_round] = P.n("ped")
                        P.wait("vector", "ped", P.n("ped"))
                        P.op("vector", lambda e, f=f, slot=slot: e.scalar_tensor_tensor(
                            out=xg[:, f, :], in0=PS[:, slot * 512:(slot + 1) * 512],
                            scalar=0.5, in1=xg[:, f, :], op0=ALU.mult, op1=ALU.add),
                            inc="ved")
                        dn_round += 1
                    P.wait("gpsimd", "ved", P.n("ved"))
                    if is3:
                        P.op("gpsimd", lambda e, gs=gs: e.dma_start(
                            out=outfm.rearrange("(j p) t -> p j t", p=128)[:, :, gs:gs + G],
                            in_=xg[:, :, :]), inc="ost", k=16)
                    else:
                        P.op("gpsimd", lambda e, gs=gs: e.dma_start(
                            out=x1d.rearrange("j p t -> p j t")[:, :, gs:gs + G],
                            in_=xg[:, :, :]), inc="x1st", k=16)

        emit_ffn(1, hook_after_ln=emit_down_up_decode_hook)

        def emit_attn():
            slR = AF[:, 8192:10240]
            slI = AF[:, 10240:12288]
            slA = AF[:, 12288:14336]
            slQ = AF[:, 14336:16384]
            slX = AF[:, 16384:18432]
            slE = AF[:, 18432:20480]
            mrep2 = slR[:, 0:512]
            msq2 = slR[:, 512:1024]
            var2 = slR[:, 1024:1536]
            rstd2 = slR[:, 1536:2048]
            xbuf2 = AF[:, 10240:11264].bitcast(BF16).rearrange(
                "p (b t) -> p b t", b=4)
            lnx2 = AH[:, 0:KD * TOK].rearrange("p (j t) -> p j t", j=KD)
            cbuf = AH[:, KD * TOK:KD * TOK + TOK]

            xg2 = AF[:, 8192:16384].rearrange("p (j t) -> p j t", j=KD)
            mrep2 = AF[:, 16384:16896]
            msq2 = AF[:, 16896:17408]
            var2 = AF[:, 17408:17920]
            rstd2 = AF[:, 17920:18432]
            xbuf2 = AF[:, 18432:19456].bitcast(BF16).rearrange(
                "p (b t) -> p b t", b=4)
            for g in range(4):
                gs = g * G
                xgb = xg if g % 2 == 0 else xg2
                if g < 2:
                    P.wait("sync", "x1st", P.n("x1st"))
                else:
                    P.wait("sync", "veln", P.n("veln") - 1)
                P.op("sync", lambda e, gs=gs, xgb=xgb: e.dma_start(
                    out=xgb[:, :, :],
                    in_=x1d.rearrange("j p t -> p j t")[:, :, gs:gs + G]),
                    inc="xld", k=16)
                P.wait("scalar", "xld", P.n("xld"))
                base_teq = P.n("teq")
                for j in range(KD):
                    if j >= 2:
                        P.wait("scalar", "teq", base_teq + 2 * (j - 1))
                    P.op("scalar", lambda e, j=j, xgb=xgb: e.copy(
                        xbuf2[:, j % 2, :], xgb[:, j, :]), inc="scq")
                    P.op("scalar", lambda e, j=j, xgb=xgb: e.activation(
                        out=xbuf2[:, 2 + j % 2, :], in_=xgb[:, j, :],
                        func=AF_.Square), inc="scq")
                P.wait("tensor", "vet")
                base_scq = P.n("scq") - 2 * KD
                for j in range(KD):
                    P.wait("tensor", "scq", base_scq + 2 * (j + 1))
                    P.op("tensor", lambda e, j=j: e.matmul(
                        PS[:, 6 * 512:7 * 512], onesb[:, :], xbuf2[:, j % 2, :],
                        start=(j == 0), stop=(j == KD - 1)), inc="teq")
                    P.op("tensor", lambda e, j=j: e.matmul(
                        PS[:, 7 * 512:8 * 512], onesb[:, :], xbuf2[:, 2 + j % 2, :],
                        start=(j == 0), stop=(j == KD - 1)),
                        inc=("pest" if j == KD - 1 else "teq"))
                P.wait("vector", "pest", P.n("pest"))
                P.op("vector", lambda e: e.tensor_scalar(
                    out=mrep2, in0=PS[:, 6 * 512:7 * 512], scalar1=INVD,
                    scalar2=None, op0=ALU.mult))
                P.op("vector", lambda e: e.tensor_tensor(
                    out=msq2, in0=mrep2, in1=mrep2, op=ALU.mult))
                P.op("vector", lambda e: e.scalar_tensor_tensor(
                    out=var2, in0=PS[:, 7 * 512:8 * 512], scalar=INVD,
                    in1=msq2, op0=ALU.mult, op1=ALU.subtract), inc="vet")
                P.wait("scalar", "vet", P.n("vet"))
                P.op("scalar", lambda e: e.activation(
                    out=msq2, in_=var2, func=AF_.Ln, bias=epspp[:, 0:1]))
                P.op("scalar", lambda e: e.activation(
                    out=rstd2, in_=msq2, func=AF_.Exp, scale=-0.5), inc="sca")
                P.wait("vector", "sca", P.n("sca"))
                tmp2 = AF[:, 19456:19968]
                for j in range(KD):
                    P.op("vector", lambda e, j=j, xgb=xgb, t=tmp2: e.tensor_tensor(
                        out=t, in0=xgb[:, j, :], in1=mrep2,
                        op=ALU.subtract))
                    P.op("vector", lambda e, j=j, gs=gs, t=tmp2: e.tensor_tensor(
                        out=lnx2[:, j, gs:gs + G], in0=t,
                        in1=rstd2, op=ALU.mult),
                        inc=("veln" if j == KD - 1 else None))
            VLN2 = P.n("veln")
            P.op("vector", lambda e: e.memset(slE[:, :], 0.0))

            P.wait("sync", "veln", VLN2)
            P.wait("tensor", "veln", VLN2)
            P.wait("tensor", "ved", P.n("ved"))
            P.wait("scalar", "veln", VLN2)
            P.wait("scalar", "vkap", 1)
            PED_A = P.n("ped")
            PERI_A = P.n("peri")
            pair_peri = {}
            for mp in range(8):
                b = mp % 2
                if mp < 2:
                    P.wait("sync", "wc", marks["wc_t"])
                    P.wait("sync", "ped", PED_A)
                else:
                    P.wait("sync", "peri", pair_peri[mp - 2])
                sem = "wsA" if b == 0 else "wsB"
                P.op("sync", lambda e, mp=mp, b=b: e.dma_start(
                    out=wg_sl[b][:, :, :],
                    in_=wabf.rearrange("p (a j c) -> p a j c", a=8, j=KD)[:, mp]),
                    inc=sem, k=16)
                P.op("sync", lambda e, mp=mp, b=b: e.dma_start(
                    out=wu_sl[b][:, :, :],
                    in_=wxbf.rearrange("p (a j c) -> p a j c", a=8, j=KD)[:, mp]),
                    inc=sem, k=16)
                val_u = P.n(sem)
                for m in (2 * mp, 2 * mp + 1):
                    col = (m % 2) * 128
                    P.wait("tensor", sem, val_u)
                    if m >= 1:
                        P.wait("tensor", "scri", P.n("scri"))
                    for k in range(KD):
                        for mat, sl in ((0, wg_sl[b]), (1, wu_sl[b])):
                            for g in range(4):
                                last = (k == KD - 1 and mat == 1 and g == 3)
                                P.op("tensor", lambda e, sl=sl, k=k, col=col,
                                     mat=mat, g=g: e.matmul(
                                         PS[:, (4 * mat + g) * 512:
                                            (4 * mat + g + 1) * 512],
                                         sl[:, k, col:col + 128],
                                         lnx2[:, k, g * G:(g + 1) * G],
                                         start=(k == 0), stop=(k == KD - 1)),
                                     inc=("peri" if last else None))
                    P.wait("scalar", "peri", P.n("peri"))
                    if m >= 1:
                        P.wait("scalar", "veu", P.n("veu"))
                        P.wait("scalar", "vesc", P.n("vesc"))
                    P.op("scalar", lambda e, m=m: e.activation(
                        out=slR[:, :], in_=PS[:, 0:2048], func=AF_.Sigmoid,
                        bias=bapp[:, m:m + 1]))
                    P.op("scalar", lambda e, m=m: e.activation(
                        out=slI[:, :], in_=PS[:, 2048:4096], func=AF_.Sigmoid,
                        bias=bxpp[:, m:m + 1]), inc="scri")
                    P.op("scalar", lambda e, m=m: e.activation(
                        out=slA[:, :], in_=slR[:, :], func=AF_.Exp,
                        scale=kap[:, m:m + 1]))
                    P.op("scalar", lambda e, m=m: e.activation(
                        out=slQ[:, :], in_=slR[:, :], func=AF_.Exp,
                        scale=kap2[:, m:m + 1]), inc="sca")
                    P.wait("vector", "sca", P.n("sca"))
                    P.op("vector", lambda e: e.tensor_scalar(
                        out=slQ[:, :], in0=slQ[:, :], scalar1=-1.0, scalar2=1.0,
                        op0=ALU.mult, op1=ALU.add))
                    P.op("vector", lambda e: e.tensor_scalar(
                        out=slQ[:, :], in0=slQ[:, :], scalar1=0.0, scalar2=None,
                        op0=ALU.max), inc="vet")
                    P.wait("scalar", "vet", P.n("vet"))
                    P.op("scalar", lambda e: e.activation(
                        out=slR[:, :], in_=slQ[:, :], func=AF_.Sqrt), inc="scsq")
                    P.wait("vector", "scsq", P.n("scsq"))
                    P.op("vector", lambda e, m=m: e.tensor_tensor(
                        out=slI[:, :], in0=slI[:, :], in1=lnx2[:, m, :],
                        op=ALU.mult))
                    P.op("vector", lambda e: e.tensor_tensor(
                        out=slI[:, :], in0=slI[:, :], in1=slR[:, :], op=ALU.mult),
                        inc="veu")
                    P.op("vector", lambda e: e.tensor_tensor_scan(
                        out=slQ[:, :], data0=slA[:, :], data1=slI[:, :],
                        initial=0.0, op0=ALU.mult, op1=ALU.add), inc="vesc")
                    P.wait("vector", "vesc", P.n("vesc"))
                    P.op("vector", lambda e, m=m: e.tensor_copy(
                        hend[:, m:m + 1], slQ[:, 2047:2048]), inc="vehc")
                    P.wait("vector", "x2st", P.n("x2st"))
                    P.op("vector", lambda e: e.tensor_tensor_scan(
                        out=cbuf[:, :], data0=slA[:, :], data1=slE[:, :],
                        initial=1.0, op0=ALU.mult, op1=ALU.add), inc="vesc")
                    P.wait("sync", "x2st", P.n("x2st"))
                    P.op("sync", lambda e, m=m: e.dma_start(
                        out=slX[:, :], in_=x1d[m, :, :]), inc="xld", k=16)
                    P.wait("vector", "xld", P.n("xld"))
                    P.op("vector", lambda e: e.tensor_tensor(
                        out=slX[:, :], in0=slX[:, :], in1=slQ[:, :], op=ALU.add),
                        inc="vesc")
                    P.wait("gpsimd", "vesc", P.n("vesc"))
                    P.op("gpsimd", lambda e, m=m: e.dma_start(
                        out=x2d[m, :, :], in_=slX[:, :]), inc="x2st", k=16)
                    P.op("gpsimd", lambda e, m=m: e.dma_start(
                        out=cd[m, :, :], in_=cbuf[:, :]), inc="x2st", k=16)
                pair_peri[mp] = P.n("peri")

            P.wait("gpsimd", "vehc", P.n("vehc"))
            P.op("gpsimd", lambda e: e.dma_start(
                out=cryl[0, :].rearrange("(p j) -> p j", p=128), in_=hend[:, :]),
                inc="hld", k=16)
            P.wait("gpsimd", "hld", P.n("hld"))
            P.wait("gpsimd", "cc", 4)
            P.op("gpsimd", lambda e: e.collective_compute(
                "AllGather", ALU.bypass,
                replica_groups=[[0, 1], [2, 3], [4, 5], [6, 7]],
                ins=[cryl.opt()], outs=[crya.opt()]), inc="cc")
            P.wait("gpsimd", "cc", P.n("cc"))
            P.op("gpsimd", lambda e: e.dma_start(
                out=hall[:, :, :],
                in_=crya.rearrange("r (p j) -> p r j", p=128)), inc="hld", k=16)
            P.wait("vector", "hld", P.n("hld"))
            P.op("vector", lambda e: e.tensor_scalar(
                out=hin[:, :], in0=hall[:, 0, :], scalar1=selpp[:, 0:1],
                scalar2=None, op0=ALU.mult))
            for r in range(1, 2):
                P.op("vector", lambda e, r=r: e.scalar_tensor_tensor(
                    out=hin[:, :], in0=hall[:, r, :], scalar=selpp[:, r:r + 1],
                    in1=hin[:, :], op0=ALU.mult, op1=ALU.add),
                    inc=("vecor" if r == 1 else None))

        emit_attn()

        emit_ffn(3)

        P.wait("gpsimd", "ost", P.n("ost"))

        block = es.enter_context(nc.Block())
        P.flush(block)

    return nc


_NC_CACHE = None


def _get_nc():
    global _NC_CACHE
    if _NC_CACHE is None:
        _NC_CACHE = build_nc()
    return _NC_CACHE


def kernel(L0_attn_x, L0_attn_Wa, L0_attn_Wx, L0_attn_ba, L0_attn_bx,
           L0_attn_Lam, L0_ffn_lut, L0_ffn_gate_idx, L0_ffn_up_idx,
           L0_ffn_down_idx):
    x = np.asarray(L0_attn_x, dtype=np.float32)
    Wa = np.asarray(L0_attn_Wa, dtype=np.float32)
    Wx = np.asarray(L0_attn_Wx, dtype=np.float32)
    ba = np.asarray(L0_attn_ba, dtype=np.float32)
    bx = np.asarray(L0_attn_bx, dtype=np.float32)
    Lam = np.asarray(L0_attn_Lam, dtype=np.float32)
    lut = np.asarray(L0_ffn_lut, dtype=np.float32)
    gidx = np.asarray(L0_ffn_gate_idx, dtype=np.int32)
    uidx = np.asarray(L0_ffn_up_idx, dtype=np.int32)
    didx = np.asarray(L0_ffn_down_idx, dtype=np.int32)

    waTh = np.ascontiguousarray(Wa.T)
    wxTh = np.ascontiguousarray(Wx.T)

    def fm(v):
        return np.ascontiguousarray(v.reshape(KD, 128).T)

    bafm, bxfm, lamfm = fm(ba), fm(bx), fm(Lam)
    lutin = lut.reshape(1, 16)

    gT = np.ascontiguousarray(gidx.T)
    uT = np.ascontiguousarray(uidx.T)
    dT = np.ascontiguousarray(didx.T)
    gTp = np.pad(gT, ((0, 0), (0, IPADC * 128 - I5)))
    uTp = np.pad(uT, ((0, 0), (0, IPADC * 128 - I5)))

    in_maps = []
    for c in range(8):
        seq, half = c // 2, c % 2
        xs = x[seq, half * TOK:(half + 1) * TOK, :]
        sel = np.zeros((1, 8), np.float32)
        if half == 1:
            sel[0, 0] = 1.0
        in_maps.append(dict(
            xin=np.ascontiguousarray(xs.T),
            waT=waTh, wxT=wxTh, bafm=bafm, bxfm=bxfm, lamfm=lamfm,
            lutin=lutin, selin=sel,
            gsh=np.ascontiguousarray(gTp[:, BCOL * c:BCOL * (c + 1)]),
            ush=np.ascontiguousarray(uTp[:, BCOL * c:BCOL * (c + 1)]),
            dsh=np.ascontiguousarray(dT[:, 256 * c:256 * (c + 1)]),
        ))

    nc = _get_nc()
    res = run_bass_kernel_spmd(nc, in_maps, core_ids=list(range(8)))

    out = np.empty((4, 4096, 2048), np.float32)
    for c in range(8):
        seq, half = c // 2, c % 2
        out[seq, half * TOK:(half + 1) * TOK, :] = np.asarray(
            res.results[c]["outfm"]).T
    return out


# version 12
# speedup vs baseline: 1.0159x; 1.0159x over previous
import contextlib
import numpy as np

import concourse.bass as bass
import concourse.mybir as mybir
from concourse.bass_utils import run_bass_kernel_spmd

F32 = mybir.dt.float32
BF16 = mybir.dt.bfloat16
I32 = mybir.dt.int32
AF_ = mybir.ActivationFunctionType
ALU = mybir.AluOpType

D = 2048
I5 = 5504
KD = 16
KI = 43
TOK = 2048
G = 512
NST = 2
IPADC = 48
BCOL = 768
UCOL = 256
GSZ = 128 * KD * BCOL
DSZ = 128 * KI * 256
SHSZ = 2 * GSZ + DSZ
EPS = 1e-5
INVD = 1.0 / D


class Prog:
    ENGS = ("sync", "gpsimd", "vector", "scalar", "tensor")

    def __init__(self):
        self.q = {e: [] for e in self.ENGS}
        self.sems = {}

    def new_sem(self, nc, es, *names):
        for n in names:
            h = es.enter_context(nc.semaphore(n))
            self.sems[n] = [h, 0]

    def n(self, name):
        return self.sems[name][1]

    def emit(self, eng, fn):
        self.q[eng].append(fn)

    def wait(self, eng, name, val=None):
        v = self.sems[name][1] if val is None else val
        if v <= 0:
            return
        h = self.sems[name][0]
        self.emit(eng, lambda e, h=h, v=v: e.wait_ge(h, v))

    def op(self, eng, build, inc=None, k=1, inc2=None, k2=1):
        h = None
        if inc is not None:
            self.sems[inc][1] += k
            h = self.sems[inc][0]
        h2 = None
        if inc2 is not None:
            self.sems[inc2][1] += k2
            h2 = self.sems[inc2][0]

        def f(e, build=build, h=h, k=k, h2=h2, k2=k2):
            ins = build(e)
            if h is not None:
                ins = ins.then_inc(h, k)
            if h2 is not None:
                ins.then_inc(h2, k2)
        self.emit(eng, f)

    def flush(self, block):
        q = self.q

        @block.sync
        def _(e):
            for f in q["sync"]:
                f(e)

        @block.gpsimd
        def _(e):
            for f in q["gpsimd"]:
                f(e)

        @block.vector
        def _(e):
            for f in q["vector"]:
                f(e)

        @block.scalar
        def _(e):
            for f in q["scalar"]:
                f(e)

        @block.tensor
        def _(e):
            for f in q["tensor"]:
                f(e)
        self.q = {e: [] for e in self.ENGS}


def build_nc():
    nc = bass.Bass("TRN2", target_bir_lowering=False, detect_race_conditions=False)

    xin = nc.dram_tensor("xin", [D, TOK], F32, kind="ExternalInput").ap()
    waT = nc.dram_tensor("waT", [D, D], F32, kind="ExternalInput").ap()
    wxT = nc.dram_tensor("wxT", [D, D], F32, kind="ExternalInput").ap()
    bafm = nc.dram_tensor("bafm", [128, KD], F32, kind="ExternalInput").ap()
    bxfm = nc.dram_tensor("bxfm", [128, KD], F32, kind="ExternalInput").ap()
    lamfm = nc.dram_tensor("lamfm", [128, KD], F32, kind="ExternalInput").ap()
    lutin = nc.dram_tensor("lutin", [1, 16], F32, kind="ExternalInput").ap()
    selin = nc.dram_tensor("selin", [1, 8], F32, kind="ExternalInput").ap()
    gsh = nc.dram_tensor("gsh", [D, BCOL], I32, kind="ExternalInput").ap()
    ush = nc.dram_tensor("ush", [D, BCOL], I32, kind="ExternalInput").ap()
    dsh = nc.dram_tensor("dsh", [I5, 256], I32, kind="ExternalInput").ap()
    outfm = nc.dram_tensor("outfm", [D, TOK], F32, kind="ExternalOutput").ap()

    wsh_ga = nc.dram_tensor("wsh_ga", [1, GSZ // 3], BF16, kind="Internal").ap()
    wfl_ga = nc.dram_tensor("wfl_ga", [8, GSZ // 3], BF16, kind="Internal").ap()
    wsh_gb = nc.dram_tensor("wsh_gb", [1, 2 * GSZ // 3], BF16, kind="Internal").ap()
    wfl_gb = nc.dram_tensor("wfl_gb", [8, 2 * GSZ // 3], BF16, kind="Internal").ap()
    wsh_u = nc.dram_tensor("wsh_u", [1, GSZ], BF16, kind="Internal").ap()
    wfl_u = nc.dram_tensor("wfl_u", [8, GSZ], BF16, kind="Internal").ap()
    wsh_d = nc.dram_tensor("wsh_d", [1, DSZ], BF16, kind="Internal").ap()
    wfl_d = nc.dram_tensor("wfl_d", [8, DSZ], BF16, kind="Internal").ap()
    wabf = nc.dram_tensor("wabf", [128, 8 * KD * 256], BF16, kind="Internal").ap()
    wxbf = nc.dram_tensor("wxbf", [128, 8 * KD * 256], BF16, kind="Internal").ap()
    x1d = nc.dram_tensor("x1d", [KD, 128, TOK], F32, kind="Internal").ap()
    x2d = nc.dram_tensor("x2d", [KD, 128, TOK], F32, kind="Internal").ap()
    cd = nc.dram_tensor("cd", [KD, 128, TOK], BF16, kind="Internal").ap()
    cryl = nc.dram_tensor("cryl", [1, 128 * KD], F32, kind="Internal").ap()
    crya = nc.dram_tensor("crya", [8, 128 * KD], F32, kind="Internal").ap()

    def gview_a(r):
        return wfl_ga[r, 0:GSZ // 3].rearrange("(p j c) -> p j c", p=128, j=KD)

    def gview_b(r):
        return wfl_gb[r, 0:2 * GSZ // 3].rearrange(
            "(p u j c) -> p u j c", p=128, u=2, j=KD)

    def uview(r):
        return wfl_u[r, 0:GSZ].rearrange("(p u j c) -> p u j c", p=128, u=3, j=KD)

    def dview(r):
        return wfl_d[r, 0:DSZ].rearrange("(p h i c) -> p h i c", p=128, h=2, i=KI)

    P = Prog()

    with contextlib.ExitStack() as es:
        sb = nc.sbuf_tensor
        lutsb = es.enter_context(sb("lutsb", [1, 16], F32))
        selsb = es.enter_context(sb("selsb", [1, 8], F32))
        ones1 = es.enter_context(sb("ones1", [1, 128], F32))
        onesb = es.enter_context(sb("onesb", [128, 128], BF16))
        kcon = es.enter_context(sb("kcon", [128, 16], F32))
        lutpp = es.enter_context(sb("lutpp", [128, 16], F32))
        selpp = es.enter_context(sb("selpp", [128, 8], F32))
        bapp = es.enter_context(sb("bapp", [128, KD], F32))
        bxpp = es.enter_context(sb("bxpp", [128, KD], F32))
        lampp = es.enter_context(sb("lampp", [128, KD], F32))
        kap = es.enter_context(sb("kap", [128, KD], F32))
        kap2 = es.enter_context(sb("kap2", [128, KD], F32))
        spt = es.enter_context(sb("spt", [128, KD], F32))
        epspp = es.enter_context(sb("epspp", [128, 1], F32))
        sps = es.enter_context(sb("sps", [128, KD], F32))
        hend = es.enter_context(sb("hend", [128, KD], F32))
        hin = es.enter_context(sb("hin", [128, KD], F32))
        hall = es.enter_context(sb("hall", [128, 8, KD], F32))
        AF = es.enter_context(sb("AF", [128, 20480], F32))
        AH = es.enter_context(sb("AH", [128, 44032], BF16))
        AW = es.enter_context(sb("AW", [128, 16384], BF16))
        PS = es.enter_context(nc.psum_tensor("PS", [128, 4096], F32))

        AFb = AF.ap().bitcast(BF16)

        P.new_sem(nc, es,
                  "prm", "vini", "pebc", "scbc", "scsp", "vkap",
                  "igg", "igu", "igu2", "igd", "wc", "ws", "cc",
                  "xld", "x1st", "x2st", "ost", "hld",
                  "wsA", "wsB",
                  "scq", "teq", "pest", "veln",
                  "sch", "veh", "ped", "ved",
                  "peri", "scri", "sca", "vet", "scsq", "veu", "vesc", "vehc",
                  "vecor")

        for dst, src in ((lutsb, lutin), (selsb, selin), (bapp, bafm),
                         (bxpp, bxfm), (lampp, lamfm)):
            P.op("sync", lambda e, d=dst, s=src: e.dma_start(out=d[:, :], in_=s[:, :]),
                 inc="prm", k=16)
        PRM_T = P.n("prm")

        for k in range(16):
            P.op("vector", lambda e, k=k: e.memset(kcon[:, k:k + 1], float(k)))
        P.op("vector", lambda e: e.memset(ones1[:, :], 1.0))
        P.op("vector", lambda e: e.memset(epspp[:, :], EPS))
        P.op("vector", lambda e: e.memset(onesb[:, :], 1.0), inc="vini")

        P.wait("tensor", "prm", PRM_T)
        P.wait("tensor", "vini", 1)
        P.op("tensor", lambda e: e.matmul(PS[:, 0:16], ones1[:, :], lutsb[:, :],
                                          start=True, stop=True), inc="pebc")
        P.op("tensor", lambda e: e.matmul(PS[:, 512:520], ones1[:, :], selsb[:, :],
                                          start=True, stop=True), inc="pebc")
        P.wait("scalar", "pebc", 2)
        P.op("scalar", lambda e: e.copy(lutpp[:, :], PS[:, 0:16]), inc="scbc")
        P.op("scalar", lambda e: e.copy(selpp[:, :], PS[:, 512:520]), inc="scbc")
        P.wait("scalar", "prm", PRM_T)
        P.op("scalar", lambda e: e.activation(out=spt[:, :], in_=lampp[:, :],
                                              func=AF_.Exp, scale=-1.0))
        P.op("scalar", lambda e: e.activation(out=sps[:, :], in_=spt[:, :],
                                              func=AF_.Ln, bias=1.0), inc="scsp")
        P.wait("vector", "scsp", 1)
        P.op("vector", lambda e: e.tensor_scalar(out=kap[:, :], in0=sps[:, :],
                                                 scalar1=-8.0, scalar2=None,
                                                 op0=ALU.mult))
        P.op("vector", lambda e: e.tensor_scalar(out=kap2[:, :], in0=sps[:, :],
                                                 scalar1=-16.0, scalar2=None,
                                                 op0=ALU.mult), inc="vkap")

        idx_g3 = AH[:, 0:KD * BCOL].rearrange("p (u j c) -> p u j c", u=3, j=KD)
        P.op("gpsimd", lambda e: e.dma_start(
            out=idx_g3, in_=gsh.rearrange("(j p) (u c) -> p u j c", p=128, u=3)),
            inc="igg", k=16)

        dec_g = AFb[:, 16384:16384 + KD * BCOL]
        msk = AW[:, 0:KD * BCOL]
        marks = {}

        def emit_decode(idx_ap, dec_ap, nfree, msk_ap, inc_last="vesc"):
            for k in range(16):
                if k == 0:
                    P.op("vector", lambda e, i=idx_ap, d=dec_ap: e.tensor_scalar(
                        out=d, in0=i, scalar1=kcon[:, 0:1], scalar2=lutpp[:, 0:1],
                        op0=ALU.is_equal, op1=ALU.mult))
                else:
                    P.op("vector", lambda e, i=idx_ap, k=k, n=nfree, m=msk_ap:
                         e.tensor_scalar(
                             out=m[:, 0:n], in0=i, scalar1=kcon[:, k:k + 1],
                             scalar2=lutpp[:, k:k + 1], op0=ALU.is_equal, op1=ALU.mult))
                    P.op("vector", lambda e, d=dec_ap, n=nfree, m=msk_ap:
                         e.tensor_tensor(out=d, in0=d, in1=m[:, 0:n], op=ALU.add),
                         inc=(inc_last if k == 15 else None))

        P.wait("vector", "scbc", 2)
        P.wait("vector", "igg", 16)
        TUG = KD * 256
        emit_decode(AH[:, 0:TUG], dec_g[:, 0:TUG], TUG, msk)
        P.wait("gpsimd", "vesc", P.n("vesc"))
        P.op("gpsimd", lambda e: e.dma_start(
            out=wsh_ga[0, :].rearrange("(p x) -> p x", p=128),
            in_=dec_g[:, 0:TUG]), inc="ws", k=16)
        P.wait("gpsimd", "ws", P.n("ws"))
        P.op("gpsimd", lambda e: e.collective_compute(
            "AllGather", ALU.bypass, replica_groups=[list(range(8))],
            ins=[wsh_ga.opt()], outs=[wfl_ga.opt()]), inc="cc")
        for uu in (1, 2):
            emit_decode(AH[:, uu * TUG:(uu + 1) * TUG],
                        dec_g[:, uu * TUG:(uu + 1) * TUG], TUG, msk)
            P.wait("gpsimd", "vesc", P.n("vesc"))
            P.op("gpsimd", lambda e, uu=uu: e.dma_start(
                out=wsh_gb[0, :].rearrange("(p x) -> p x", p=128)[
                    :, (uu - 1) * TUG:uu * TUG],
                in_=dec_g[:, uu * TUG:(uu + 1) * TUG]), inc="ws", k=16)
        P.wait("gpsimd", "ws", P.n("ws"))
        P.op("gpsimd", lambda e: e.collective_compute(
            "AllGather", ALU.bypass, replica_groups=[list(range(8))],
            ins=[wsh_gb.opt()], outs=[wfl_gb.opt()]), inc="cc")
        marks["ws_g"] = P.n("ws")

        wshd_pm = wsh_d[0, 0:DSZ].rearrange("(p x) -> p x", p=128)
        wshu_pm = wsh_u[0, 0:GSZ].rearrange("(p x) -> p x", p=128)
        QD = 2752
        HU = KD * BCOL // 2

        def emit_down_up_decode_hook():
            TU = KD * 256
            uview_in = ush.rearrange("(j p) (u c) -> p u j c", p=128, u=3)
            t_vesc = {}
            for tt in range(3):
                aw_sl = 8192 + (tt % 2) * TU
                dec_sl = (tt % 2) * TU
                usem = "igu" if tt % 2 == 0 else "igu2"
                if tt >= 2:
                    P.wait("gpsimd", "vesc", t_vesc[tt - 2])
                P.op("gpsimd", lambda e, sl=aw_sl, tt=tt: e.dma_start(
                    out=AW[:, sl:sl + TU], in_=uview_in[:, tt]), inc=usem, k=16)
                idx_t = AW[:, aw_sl:aw_sl + TU]
                dec_t = AFb[:, dec_sl:dec_sl + TU]
                msk_t = AFb[:, 2 * TU:3 * TU]
                P.wait("vector", usem, P.n(usem))
                if tt >= 2:
                    P.wait("vector", "ws", P.n("ws"))
                emit_decode(idx_t, dec_t, TU, msk_t)
                t_vesc[tt] = P.n("vesc")
                P.wait("gpsimd", "vesc", P.n("vesc"))
                P.op("gpsimd", lambda e, tt=tt, sl=dec_sl: e.dma_start(
                    out=wshu_pm[:, tt * TU:(tt + 1) * TU],
                    in_=AFb[:, sl:sl + TU]), inc="ws", k=16)
            P.wait("gpsimd", "ws", P.n("ws"))
            P.op("gpsimd", lambda e: e.collective_compute(
                "AllGather", ALU.bypass, replica_groups=[list(range(8))],
                ins=[wsh_u.opt()], outs=[wfl_u.opt()]), inc="cc")
            HD = KI * 128
            dview_in = dsh.rearrange("(i p) (h c) -> p h i c", p=128, h=2)
            for hh in range(2):
                P.wait("gpsimd", "vesc", P.n("vesc"))
                P.op("gpsimd", lambda e, hh=hh: e.dma_start(
                    out=AW[:, 8192:8192 + HD], in_=dview_in[:, hh]),
                    inc="igd", k=16)
                P.wait("vector", "igd", P.n("igd"))
                P.wait("vector", "ws", P.n("ws"))
                emit_decode(AW[:, 8192:8192 + HD], AFb[:, 0:HD], HD,
                            AFb[:, HD:2 * HD])
                P.wait("gpsimd", "vesc", P.n("vesc"))
                P.op("gpsimd", lambda e, hh=hh: e.dma_start(
                    out=wshd_pm[:, hh * HD:(hh + 1) * HD],
                    in_=AFb[:, 0:HD]), inc="ws", k=16)
            P.wait("gpsimd", "ws", P.n("ws"))
            P.op("gpsimd", lambda e: e.collective_compute(
                "AllGather", ALU.bypass, replica_groups=[list(range(8))],
                ins=[wsh_d.opt()], outs=[wfl_d.opt()]), inc="cc")
            marks["ws_dec"] = P.n("ws")
            P.wait("vector", "ws", P.n("ws"))
            for aa in range(8):
                P.op("gpsimd", lambda e, a=aa: e.dma_start(
                    out=wabf.rearrange("p (a j c) -> p a j c", a=8, j=KD)[:, a],
                    in_=waT.rearrange("(j p) (a c) -> p a j c", p=128, a=8)[:, a]),
                    inc="wc", k=16)
                P.op("gpsimd", lambda e, a=aa: e.dma_start(
                    out=wxbf.rearrange("p (a j c) -> p a j c", a=8, j=KD)[:, a],
                    in_=wxT.rearrange("(j p) (a c) -> p a j c", p=128, a=8)[:, a]),
                    inc="wc", k=16)
            marks["wc_t"] = P.n("wc")

        xg = AF[:, 0:8192].rearrange("p (j t) -> p j t", j=KD)
        lnx = AFb[:, 16384:32768].rearrange("p (j t) -> p j t", j=KD)
        stmp = AF[:, 16384:17408]
        mrep = AF[:, 17408:17920]
        msq = AF[:, 17920:18432]
        var = AF[:, 18432:18944]
        rstd = AF[:, 18944:19456]
        xbuf = AFb[:, 38912:40960].rearrange("p (b t) -> p b t", b=4)
        h_all = AH[:, 0:KI * 1024].rearrange("p (i t) -> p i t", i=KI)
        wg_sl = [AW[:, 0:4096].rearrange("p (j c) -> p j c", j=KD),
                 AW[:, 4096:8192].rearrange("p (j c) -> p j c", j=KD)]
        wu_sl = [AW[:, 8192:12288].rearrange("p (j c) -> p j c", j=KD),
                 AW[:, 12288:16384].rearrange("p (j c) -> p j c", j=KD)]
        wd_sl = [AW[:, 0:5504].rearrange("p (i c) -> p i c", i=KI),
                 AW[:, 5504:11008].rearrange("p (i c) -> p i c", i=KI)]

        def emit_correction(cgrp):
            for j in range(KD):
                P.op("vector", lambda e, j=j, c=cgrp: e.scalar_tensor_tensor(
                    out=xg[:, j, :], in0=c[:, j, :], scalar=hin[:, j:j + 1],
                    in1=xg[:, j, :], op0=ALU.mult, op1=ALU.add),
                    inc=("vecor" if j == KD - 1 else None))

        def emit_group_ln(g2):
            base_teq = P.n("teq")
            for j in range(KD):
                if j >= 2:
                    P.wait("scalar", "teq", base_teq + 2 * (j - 1))
                P.op("scalar", lambda e, j=j: e.copy(
                    xbuf[:, j % 2, :], xg[:, j, :]), inc="scq")
                P.op("scalar", lambda e, j=j: e.activation(
                    out=xbuf[:, 2 + j % 2, :], in_=xg[:, j, :],
                    func=AF_.Square), inc="scq")
            P.wait("tensor", "vet")
            base_scq = P.n("scq") - 2 * KD
            for j in range(KD):
                P.wait("tensor", "scq", base_scq + 2 * (j + 1))
                P.op("tensor", lambda e, j=j: e.matmul(
                    PS[:, 6 * 512:7 * 512], onesb[:, :], xbuf[:, j % 2, :],
                    start=(j == 0), stop=(j == KD - 1)), inc="teq")
                P.op("tensor", lambda e, j=j: e.matmul(
                    PS[:, 7 * 512:8 * 512], onesb[:, :], xbuf[:, 2 + j % 2, :],
                    start=(j == 0), stop=(j == KD - 1)),
                    inc=("pest" if j == KD - 1 else "teq"))
            P.wait("vector", "pest", P.n("pest"))
            P.op("vector", lambda e: e.tensor_scalar(
                out=mrep[:, :], in0=PS[:, 6 * 512:7 * 512], scalar1=INVD,
                scalar2=None, op0=ALU.mult))
            P.op("vector", lambda e: e.tensor_tensor(
                out=msq[:, :], in0=mrep[:, :], in1=mrep[:, :], op=ALU.mult))
            P.op("vector", lambda e: e.scalar_tensor_tensor(
                out=var[:, :], in0=PS[:, 7 * 512:8 * 512], scalar=INVD,
                in1=msq[:, :], op0=ALU.mult, op1=ALU.subtract), inc="vet")
            P.wait("scalar", "vet", P.n("vet"))
            P.op("scalar", lambda e: e.activation(
                out=msq[:, :], in_=var[:, :], func=AF_.Ln, bias=epspp[:, 0:1]))
            P.op("scalar", lambda e: e.activation(
                out=rstd[:, :], in_=msq[:, :], func=AF_.Exp, scale=-0.5), inc="sca")
            P.wait("vector", "sca", P.n("sca"))
            for j in range(KD):
                P.op("vector", lambda e, j=j: e.tensor_tensor(
                    out=stmp[:, 0:512], in0=xg[:, j, :], in1=mrep[:, :],
                    op=ALU.subtract))
                P.op("vector", lambda e, j=j, g2=g2: e.tensor_tensor(
                    out=lnx[:, j, g2 * G:(g2 + 1) * G], in0=stmp[:, 0:512],
                    in1=rstd[:, :], op=ALU.mult),
                    inc=("veln" if j == KD - 1 else None))

        def emit_ffn(pass_id, hook_after_ln=None):
            is3 = (pass_id == 3)
            st_sem = "ost" if is3 else "x1st"
            for st in range(NST):
                for g2 in range(2):
                    g = 2 * st + g2
                    gs = g * G
                    P.wait("sync", "veln")
                    P.wait("sync", st_sem, P.n(st_sem))
                    if is3:
                        P.wait("sync", "vecor")
                        P.wait("sync", "x2st", P.n("x2st"))
                        cgrp = lnx[:, :, g2 * G:(g2 + 1) * G]
                        P.op("sync", lambda e, gs=gs: e.dma_start(
                            out=xg[:, :, :],
                            in_=x2d.rearrange("j p t -> p j t")[:, :, gs:gs + G]),
                            inc="xld", k=16)
                        P.op("sync", lambda e, gs=gs, c=cgrp: e.dma_start(
                            out=c, in_=cd.rearrange("j p t -> p j t")[:, :, gs:gs + G]),
                            inc="xld", k=16)
                        P.wait("vector", "xld", P.n("xld"))
                        emit_correction(cgrp)
                        P.wait("scalar", "vecor", P.n("vecor"))
                    else:
                        P.op("sync", lambda e, gs=gs: e.dma_start(
                            out=xg[:, :, :],
                            in_=xin.rearrange("(j p) t -> p j t", p=128)[:, :, gs:gs + G]),
                            inc="xld", k=16)
                        P.wait("scalar", "xld", P.n("xld"))
                    if not is3 and st == 0:
                        P.wait("vector", "ws", marks["ws_g"])
                    emit_group_ln(g2)
                if hook_after_ln is not None and st == 0:
                    hook_after_ln()
                    hook_after_ln = None
                VLN = P.n("veln")

                P.wait("tensor", "veln", VLN)
                P.wait("tensor", "cc", 1)
                P.wait("tensor", "scbc", 2)
                PED0 = P.n("ped")
                PERI0 = P.n("peri")
                unit_ped = {}
                it_n = 0
                gate_units = [(uu, r) for uu in range(3) for r in range(8)
                              if 6 * r + 2 * uu <= 42]
                for u, (uu, r) in enumerate(gate_units):
                    b = u % 2
                    if u < 2:
                        P.wait("sync", "ped", PED0)
                        P.wait("sync", "peri", PERI0)
                    else:
                        P.wait("sync", "ped", unit_ped[u - 2])
                    if uu == 0 and r == 0:
                        P.wait("sync", "cc", 1)
                    if uu == 1 and r == 0:
                        P.wait("sync", "cc", 2)
                    sem = "wsA" if b == 0 else "wsB"
                    if uu == 0:
                        P.op("sync", lambda e, r=r, b=b: e.dma_start(
                            out=wg_sl[b][:, :, :], in_=gview_a(r)), inc=sem, k=16)
                    else:
                        P.op("sync", lambda e, r=r, uu=uu, b=b: e.dma_start(
                            out=wg_sl[b][:, :, :], in_=gview_b(r)[:, uu - 1]),
                            inc=sem, k=16)
                    val_u = P.n(sem)
                    for mi, m in enumerate((6 * r + 2 * uu, 6 * r + 2 * uu + 1)):
                        if m >= KI:
                            continue
                        col = mi * 128
                        pp = it_n % 2
                        it_n += 1
                        P.wait("tensor", sem, val_u)
                        if it_n > 2:
                            P.wait("tensor", "sch", P.n("sch") - 1)
                        for k in range(KD):
                            for g2 in range(2):
                                bk = 2 * pp + g2
                                last = (k == KD - 1 and g2 == 1)
                                P.op("tensor",
                                     lambda e, b=b, k=k, col=col, bk=bk,
                                     g2=g2: e.matmul(
                                         PS[:, bk * 512:(bk + 1) * 512],
                                         wg_sl[b][:, k, col:col + 128],
                                         lnx[:, k, g2 * G:(g2 + 1) * G],
                                         start=(k == 0), stop=(k == KD - 1)),
                                     inc=("ped" if last else None))
                        P.wait("scalar", "ped", P.n("ped"))
                        P.op("scalar", lambda e, m=m, pp=pp: e.activation(
                            out=h_all[:, m, :],
                            in_=PS[:, (2 * pp) * 512:(2 * pp + 2) * 512],
                            func=AF_.Silu), inc="sch")
                    unit_ped[u] = P.n("ped")

                P.wait("sync", "cc", 3 if not is3 else 0)
                unit_ped_u = {}
                for u in range(22):
                    b = u % 2
                    r, uu = u // 3, u % 3
                    if u < 2:
                        P.wait("sync", "ped", unit_ped[20 + u])
                        if u == 0 and not is3 and st == 0 and "ws_dec" in marks:
                            P.wait("sync", "ws", marks["ws_dec"])
                    else:
                        P.wait("sync", "ped", unit_ped_u[u - 2])
                    sem = "wsA" if b == 0 else "wsB"
                    P.op("sync", lambda e, r=r, uu=uu, b=b: e.dma_start(
                        out=wu_sl[b][:, :, :], in_=uview(r)[:, uu]), inc=sem, k=16)
                    val_u = P.n(sem)
                    for m in (2 * u, 2 * u + 1):
                        if m >= KI:
                            continue
                        col = (m % 2) * 128
                        pp = m % 2
                        P.wait("tensor", sem, val_u)
                        if m >= 2:
                            P.wait("tensor", "veh", P.n("veh") - 1)
                        elif u == 0:
                            P.wait("tensor", "sch", P.n("sch"))
                        for k in range(KD):
                            for g2 in range(2):
                                bk = 4 + 2 * pp + g2
                                last = (k == KD - 1 and g2 == 1)
                                P.op("tensor",
                                     lambda e, b=b, k=k, col=col, bk=bk,
                                     g2=g2: e.matmul(
                                         PS[:, bk * 512:(bk + 1) * 512],
                                         wu_sl[b][:, k, col:col + 128],
                                         lnx[:, k, g2 * G:(g2 + 1) * G],
                                         start=(k == 0), stop=(k == KD - 1)),
                                     inc=("ped" if last else None))
                        P.wait("vector", "ped", P.n("ped"))
                        P.op("vector", lambda e, m=m, pp=pp: e.tensor_tensor(
                            out=h_all[:, m, :], in0=h_all[:, m, :],
                            in1=PS[:, (4 + 2 * pp) * 512:(4 + 2 * pp + 2) * 512],
                            op=ALU.mult), inc="veh")
                    unit_ped_u[u] = P.n("ped")

                GU_VEH = P.n("veh")
                PED_DN = P.n("ped")
                round_ped = {}
                dn_round = 0
                for g2 in range(2):
                    g = 2 * st + g2
                    gs = g * G
                    P.wait("sync", "veln", VLN)
                    if not is3 and st == 0 and g2 == 0 and "ws_dec" in marks:
                        P.wait("sync", "ws", marks["ws_dec"])
                    if g2 == 1:
                        P.wait("sync", st_sem, P.n(st_sem))
                    if is3:
                        P.wait("sync", "vecor")
                        P.wait("sync", "ped", PED_DN)
                        cgrp = lnx[:, :, g2 * G:(g2 + 1) * G]
                        P.op("sync", lambda e, gs=gs: e.dma_start(
                            out=xg[:, :, :],
                            in_=x2d.rearrange("j p t -> p j t")[:, :, gs:gs + G]),
                            inc="xld", k=16)
                        P.op("sync", lambda e, gs=gs, c=cgrp: e.dma_start(
                            out=c, in_=cd.rearrange("j p t -> p j t")[:, :, gs:gs + G]),
                            inc="xld", k=16)
                        P.wait("vector", "xld", P.n("xld"))
                        emit_correction(cgrp)
                    else:
                        P.op("sync", lambda e, gs=gs: e.dma_start(
                            out=xg[:, :, :],
                            in_=xin.rearrange("(j p) t -> p j t", p=128)[:, :, gs:gs + G]),
                            inc="xld", k=16)
                        P.wait("vector", "xld", P.n("xld"))
                    for f in range(KD):
                        db = dn_round % 2
                        rb, ch = f // 2, f % 2
                        if dn_round < 2:
                            P.wait("sync", "ped", PED_DN)
                        else:
                            P.wait("sync", "ped", round_ped[dn_round - 2])
                        sem = "wsA" if db == 0 else "wsB"
                        if dn_round == 0 and st == 0 and pass_id == 1:
                            P.wait("sync", "cc", 4)
                        P.op("sync", lambda e, rb=rb, ch=ch, db=db: e.dma_start(
                            out=wd_sl[db][:, :, :],
                            in_=dview(rb)[:, ch]), inc=sem, k=16)
                        val_u = P.n(sem)
                        P.wait("tensor", sem, val_u)
                        if g2 == 0 and f == 0:
                            P.wait("tensor", "veh", GU_VEH)
                        slot = dn_round % 4
                        if dn_round >= 4:
                            P.wait("tensor", "ved", P.n("ved") - 3)
                        for i in range(KI):
                            P.op("tensor", lambda e, i=i, db=db, slot=slot,
                                 g2=g2: e.matmul(
                                     PS[:, slot * 512:(slot + 1) * 512],
                                     wd_sl[db][:, i, :],
                                     h_all[:, i, g2 * G:(g2 + 1) * G],
                                     start=(i == 0), stop=(i == KI - 1)),
                                 inc=("ped" if i == KI - 1 else None))
                        round_ped[dn_round] = P.n("ped")
                        P.wait("vector", "ped", P.n("ped"))
                        P.op("vector", lambda e, f=f, slot=slot: e.scalar_tensor_tensor(
                            out=xg[:, f, :], in0=PS[:, slot * 512:(slot + 1) * 512],
                            scalar=0.5, in1=xg[:, f, :], op0=ALU.mult, op1=ALU.add),
                            inc="ved")
                        P.wait("gpsimd", "ved", P.n("ved"))
                        if is3:
                            P.op("gpsimd", lambda e, gs=gs, f=f: e.dma_start(
                                out=outfm.rearrange("(j p) t -> p j t",
                                                    p=128)[:, f, gs:gs + G],
                                in_=xg[:, f, :]), inc="ost", k=16)
                        else:
                            P.op("gpsimd", lambda e, gs=gs, f=f: e.dma_start(
                                out=x1d.rearrange("j p t -> p j t")[:, f,
                                                                    gs:gs + G],
                                in_=xg[:, f, :]), inc="x1st", k=16)
                        dn_round += 1

        emit_ffn(1, hook_after_ln=emit_down_up_decode_hook)

        def emit_attn():
            slR = AF[:, 8192:10240]
            slI = AF[:, 10240:12288]
            slA = AF[:, 12288:14336]
            slQ = AF[:, 14336:16384]
            slX = AF[:, 16384:18432]
            slE = AF[:, 18432:20480]
            mrep2 = slR[:, 0:512]
            msq2 = slR[:, 512:1024]
            var2 = slR[:, 1024:1536]
            rstd2 = slR[:, 1536:2048]
            xbuf2 = AF[:, 10240:11264].bitcast(BF16).rearrange(
                "p (b t) -> p b t", b=4)
            lnx2 = AH[:, 0:KD * TOK].rearrange("p (j t) -> p j t", j=KD)
            cbuf = AH[:, KD * TOK:KD * TOK + TOK]

            xg2 = AF[:, 8192:16384].rearrange("p (j t) -> p j t", j=KD)
            mrep2 = AF[:, 16384:16896]
            msq2 = AF[:, 16896:17408]
            var2 = AF[:, 17408:17920]
            rstd2 = AF[:, 17920:18432]
            xbuf2 = AF[:, 18432:19456].bitcast(BF16).rearrange(
                "p (b t) -> p b t", b=4)
            for g in range(4):
                gs = g * G
                xgb = xg if g % 2 == 0 else xg2
                if g < 2:
                    P.wait("sync", "x1st", P.n("x1st"))
                else:
                    P.wait("sync", "veln", P.n("veln") - 1)
                P.op("sync", lambda e, gs=gs, xgb=xgb: e.dma_start(
                    out=xgb[:, :, :],
                    in_=x1d.rearrange("j p t -> p j t")[:, :, gs:gs + G]),
                    inc="xld", k=16)
                P.wait("scalar", "xld", P.n("xld"))
                base_teq = P.n("teq")
                for j in range(KD):
                    if j >= 2:
                        P.wait("scalar", "teq", base_teq + 2 * (j - 1))
                    P.op("scalar", lambda e, j=j, xgb=xgb: e.copy(
                        xbuf2[:, j % 2, :], xgb[:, j, :]), inc="scq")
                    P.op("scalar", lambda e, j=j, xgb=xgb: e.activation(
                        out=xbuf2[:, 2 + j % 2, :], in_=xgb[:, j, :],
                        func=AF_.Square), inc="scq")
                P.wait("tensor", "vet")
                base_scq = P.n("scq") - 2 * KD
                for j in range(KD):
                    P.wait("tensor", "scq", base_scq + 2 * (j + 1))
                    P.op("tensor", lambda e, j=j: e.matmul(
                        PS[:, 6 * 512:7 * 512], onesb[:, :], xbuf2[:, j % 2, :],
                        start=(j == 0), stop=(j == KD - 1)), inc="teq")
                    P.op("tensor", lambda e, j=j: e.matmul(
                        PS[:, 7 * 512:8 * 512], onesb[:, :], xbuf2[:, 2 + j % 2, :],
                        start=(j == 0), stop=(j == KD - 1)),
                        inc=("pest" if j == KD - 1 else "teq"))
                P.wait("vector", "pest", P.n("pest"))
                P.op("vector", lambda e: e.tensor_scalar(
                    out=mrep2, in0=PS[:, 6 * 512:7 * 512], scalar1=INVD,
                    scalar2=None, op0=ALU.mult))
                P.op("vector", lambda e: e.tensor_tensor(
                    out=msq2, in0=mrep2, in1=mrep2, op=ALU.mult))
                P.op("vector", lambda e: e.scalar_tensor_tensor(
                    out=var2, in0=PS[:, 7 * 512:8 * 512], scalar=INVD,
                    in1=msq2, op0=ALU.mult, op1=ALU.subtract), inc="vet")
                P.wait("scalar", "vet", P.n("vet"))
                P.op("scalar", lambda e: e.activation(
                    out=msq2, in_=var2, func=AF_.Ln, bias=epspp[:, 0:1]))
                P.op("scalar", lambda e: e.activation(
                    out=rstd2, in_=msq2, func=AF_.Exp, scale=-0.5), inc="sca")
                P.wait("vector", "sca", P.n("sca"))
                tmp2 = AF[:, 19456:19968]
                for j in range(KD):
                    P.op("vector", lambda e, j=j, xgb=xgb, t=tmp2: e.tensor_tensor(
                        out=t, in0=xgb[:, j, :], in1=mrep2,
                        op=ALU.subtract))
                    P.op("vector", lambda e, j=j, gs=gs, t=tmp2: e.tensor_tensor(
                        out=lnx2[:, j, gs:gs + G], in0=t,
                        in1=rstd2, op=ALU.mult),
                        inc=("veln" if j == KD - 1 else None))
            VLN2 = P.n("veln")
            P.op("vector", lambda e: e.memset(slE[:, :], 0.0))

            P.wait("sync", "veln", VLN2)
            P.wait("tensor", "veln", VLN2)
            P.wait("tensor", "ved", P.n("ved"))
            P.wait("scalar", "veln", VLN2)
            P.wait("scalar", "vkap", 1)
            PED_A = P.n("ped")
            PERI_A = P.n("peri")
            pair_peri = {}
            for mp in range(8):
                b = mp % 2
                if mp < 2:
                    P.wait("sync", "wc", marks["wc_t"])
                    P.wait("sync", "ped", PED_A)
                else:
                    P.wait("sync", "peri", pair_peri[mp - 2])
                sem = "wsA" if b == 0 else "wsB"
                P.op("sync", lambda e, mp=mp, b=b: e.dma_start(
                    out=wg_sl[b][:, :, :],
                    in_=wabf.rearrange("p (a j c) -> p a j c", a=8, j=KD)[:, mp]),
                    inc=sem, k=16)
                P.op("sync", lambda e, mp=mp, b=b: e.dma_start(
                    out=wu_sl[b][:, :, :],
                    in_=wxbf.rearrange("p (a j c) -> p a j c", a=8, j=KD)[:, mp]),
                    inc=sem, k=16)
                val_u = P.n(sem)
                for m in (2 * mp, 2 * mp + 1):
                    col = (m % 2) * 128
                    P.wait("tensor", sem, val_u)
                    if m >= 1:
                        P.wait("tensor", "scri", P.n("scri"))
                    for k in range(KD):
                        for mat, sl in ((0, wg_sl[b]), (1, wu_sl[b])):
                            for g in range(4):
                                last = (k == KD - 1 and mat == 1 and g == 3)
                                P.op("tensor", lambda e, sl=sl, k=k, col=col,
                                     mat=mat, g=g: e.matmul(
                                         PS[:, (4 * mat + g) * 512:
                                            (4 * mat + g + 1) * 512],
                                         sl[:, k, col:col + 128],
                                         lnx2[:, k, g * G:(g + 1) * G],
                                         start=(k == 0), stop=(k == KD - 1)),
                                     inc=("peri" if last else None))
                    P.wait("scalar", "peri", P.n("peri"))
                    if m >= 1:
                        P.wait("scalar", "veu", P.n("veu"))
                        P.wait("scalar", "vesc", P.n("vesc"))
                    P.op("scalar", lambda e, m=m: e.activation(
                        out=slR[:, :], in_=PS[:, 0:2048], func=AF_.Sigmoid,
                        bias=bapp[:, m:m + 1]))
                    P.op("scalar", lambda e, m=m: e.activation(
                        out=slI[:, :], in_=PS[:, 2048:4096], func=AF_.Sigmoid,
                        bias=bxpp[:, m:m + 1]), inc="scri")
                    P.op("scalar", lambda e, m=m: e.activation(
                        out=slA[:, :], in_=slR[:, :], func=AF_.Exp,
                        scale=kap[:, m:m + 1]))
                    P.op("scalar", lambda e, m=m: e.activation(
                        out=slQ[:, :], in_=slR[:, :], func=AF_.Exp,
                        scale=kap2[:, m:m + 1]), inc="sca")
                    P.wait("vector", "sca", P.n("sca"))
                    P.op("vector", lambda e: e.tensor_scalar(
                        out=slQ[:, :], in0=slQ[:, :], scalar1=-1.0, scalar2=1.0,
                        op0=ALU.mult, op1=ALU.add))
                    P.op("vector", lambda e: e.tensor_scalar(
                        out=slQ[:, :], in0=slQ[:, :], scalar1=0.0, scalar2=None,
                        op0=ALU.max), inc="vet")
                    P.wait("scalar", "vet", P.n("vet"))
                    P.op("scalar", lambda e: e.activation(
                        out=slR[:, :], in_=slQ[:, :], func=AF_.Sqrt), inc="scsq")
                    P.wait("vector", "scsq", P.n("scsq"))
                    P.op("vector", lambda e, m=m: e.tensor_tensor(
                        out=slI[:, :], in0=slI[:, :], in1=lnx2[:, m, :],
                        op=ALU.mult))
                    P.op("vector", lambda e: e.tensor_tensor(
                        out=slI[:, :], in0=slI[:, :], in1=slR[:, :], op=ALU.mult),
                        inc="veu")
                    P.op("vector", lambda e: e.tensor_tensor_scan(
                        out=slQ[:, :], data0=slA[:, :], data1=slI[:, :],
                        initial=0.0, op0=ALU.mult, op1=ALU.add), inc="vesc")
                    P.wait("vector", "vesc", P.n("vesc"))
                    P.op("vector", lambda e, m=m: e.tensor_copy(
                        hend[:, m:m + 1], slQ[:, 2047:2048]), inc="vehc")
                    P.wait("vector", "x2st", P.n("x2st"))
                    P.op("vector", lambda e: e.tensor_tensor_scan(
                        out=cbuf[:, :], data0=slA[:, :], data1=slE[:, :],
                        initial=1.0, op0=ALU.mult, op1=ALU.add), inc="vesc")
                    P.wait("sync", "x2st", P.n("x2st"))
                    P.op("sync", lambda e, m=m: e.dma_start(
                        out=slX[:, :], in_=x1d[m, :, :]), inc="xld", k=16)
                    P.wait("vector", "xld", P.n("xld"))
                    P.op("vector", lambda e: e.tensor_tensor(
                        out=slX[:, :], in0=slX[:, :], in1=slQ[:, :], op=ALU.add),
                        inc="vesc")
                    P.wait("gpsimd", "vesc", P.n("vesc"))
                    P.op("gpsimd", lambda e, m=m: e.dma_start(
                        out=x2d[m, :, :], in_=slX[:, :]), inc="x2st", k=16)
                    P.op("gpsimd", lambda e, m=m: e.dma_start(
                        out=cd[m, :, :], in_=cbuf[:, :]), inc="x2st", k=16)
                pair_peri[mp] = P.n("peri")

            P.wait("gpsimd", "vehc", P.n("vehc"))
            P.op("gpsimd", lambda e: e.dma_start(
                out=cryl[0, :].rearrange("(p j) -> p j", p=128), in_=hend[:, :]),
                inc="hld", k=16)
            P.wait("gpsimd", "hld", P.n("hld"))
            P.wait("gpsimd", "cc", 2)
            P.op("gpsimd", lambda e: e.collective_compute(
                "AllGather", ALU.bypass, replica_groups=[list(range(8))],
                ins=[cryl.opt()], outs=[crya.opt()]), inc="cc")
            P.wait("gpsimd", "cc", P.n("cc"))
            P.op("gpsimd", lambda e: e.dma_start(
                out=hall[:, :, :],
                in_=crya.rearrange("r (p j) -> p r j", p=128)), inc="hld", k=16)
            P.wait("vector", "hld", P.n("hld"))
            P.op("vector", lambda e: e.tensor_scalar(
                out=hin[:, :], in0=hall[:, 0, :], scalar1=selpp[:, 0:1],
                scalar2=None, op0=ALU.mult))
            for r in range(1, 8):
                P.op("vector", lambda e, r=r: e.scalar_tensor_tensor(
                    out=hin[:, :], in0=hall[:, r, :], scalar=selpp[:, r:r + 1],
                    in1=hin[:, :], op0=ALU.mult, op1=ALU.add),
                    inc=("vecor" if r == 7 else None))

        emit_attn()

        emit_ffn(3)

        P.wait("gpsimd", "ost", P.n("ost"))

        block = es.enter_context(nc.Block())
        P.flush(block)

    return nc


_NC_CACHE = None


def _get_nc():
    global _NC_CACHE
    if _NC_CACHE is None:
        _NC_CACHE = build_nc()
    return _NC_CACHE


def kernel(L0_attn_x, L0_attn_Wa, L0_attn_Wx, L0_attn_ba, L0_attn_bx,
           L0_attn_Lam, L0_ffn_lut, L0_ffn_gate_idx, L0_ffn_up_idx,
           L0_ffn_down_idx):
    x = np.asarray(L0_attn_x, dtype=np.float32)
    Wa = np.asarray(L0_attn_Wa, dtype=np.float32)
    Wx = np.asarray(L0_attn_Wx, dtype=np.float32)
    ba = np.asarray(L0_attn_ba, dtype=np.float32)
    bx = np.asarray(L0_attn_bx, dtype=np.float32)
    Lam = np.asarray(L0_attn_Lam, dtype=np.float32)
    lut = np.asarray(L0_ffn_lut, dtype=np.float32)
    gidx = np.asarray(L0_ffn_gate_idx, dtype=np.int32)
    uidx = np.asarray(L0_ffn_up_idx, dtype=np.int32)
    didx = np.asarray(L0_ffn_down_idx, dtype=np.int32)

    waTh = np.ascontiguousarray(Wa.T)
    wxTh = np.ascontiguousarray(Wx.T)

    def fm(v):
        return np.ascontiguousarray(v.reshape(KD, 128).T)

    bafm, bxfm, lamfm = fm(ba), fm(bx), fm(Lam)
    lutin = lut.reshape(1, 16)

    gT = np.ascontiguousarray(gidx.T)
    uT = np.ascontiguousarray(uidx.T)
    dT = np.ascontiguousarray(didx.T)
    gTp = np.pad(gT, ((0, 0), (0, IPADC * 128 - I5)))
    uTp = np.pad(uT, ((0, 0), (0, IPADC * 128 - I5)))

    in_maps = []
    for c in range(8):
        seq, half = c // 2, c % 2
        xs = x[seq, half * TOK:(half + 1) * TOK, :]
        sel = np.zeros((1, 8), np.float32)
        if half == 1:
            sel[0, c - 1] = 1.0
        in_maps.append(dict(
            xin=np.ascontiguousarray(xs.T),
            waT=waTh, wxT=wxTh, bafm=bafm, bxfm=bxfm, lamfm=lamfm,
            lutin=lutin, selin=sel,
            gsh=np.ascontiguousarray(gTp[:, BCOL * c:BCOL * (c + 1)]),
            ush=np.ascontiguousarray(uTp[:, BCOL * c:BCOL * (c + 1)]),
            dsh=np.ascontiguousarray(dT[:, 256 * c:256 * (c + 1)]),
        ))

    nc = _get_nc()
    res = run_bass_kernel_spmd(nc, in_maps, core_ids=list(range(8)))

    out = np.empty((4, 4096, 2048), np.float32)
    for c in range(8):
        seq, half = c // 2, c % 2
        out[seq, half * TOK:(half + 1) * TOK, :] = np.asarray(
            res.results[c]["outfm"]).T
    return out


# version 13
# speedup vs baseline: 1.0308x; 1.0146x over previous
import contextlib
import numpy as np

import concourse.bass as bass
import concourse.mybir as mybir
from concourse.bass_utils import run_bass_kernel_spmd

F32 = mybir.dt.float32
BF16 = mybir.dt.bfloat16
I32 = mybir.dt.int32
AF_ = mybir.ActivationFunctionType
ALU = mybir.AluOpType

D = 2048
I5 = 5504
KD = 16
KI = 43
TOK = 2048
G = 512
NST = 2
IPADC = 48
BCOL = 768
UCOL = 256
GSZ = 128 * KD * BCOL
DSZ = 128 * KI * 256
SHSZ = 2 * GSZ + DSZ
EPS = 1e-5
INVD = 1.0 / D


class Prog:
    ENGS = ("sync", "gpsimd", "vector", "scalar", "tensor")

    def __init__(self):
        self.q = {e: [] for e in self.ENGS}
        self.sems = {}

    def new_sem(self, nc, es, *names):
        for n in names:
            h = es.enter_context(nc.semaphore(n))
            self.sems[n] = [h, 0]

    def n(self, name):
        return self.sems[name][1]

    def emit(self, eng, fn):
        self.q[eng].append(fn)

    def wait(self, eng, name, val=None):
        v = self.sems[name][1] if val is None else val
        if v <= 0:
            return
        h = self.sems[name][0]
        self.emit(eng, lambda e, h=h, v=v: e.wait_ge(h, v))

    def op(self, eng, build, inc=None, k=1, inc2=None, k2=1):
        h = None
        if inc is not None:
            self.sems[inc][1] += k
            h = self.sems[inc][0]
        h2 = None
        if inc2 is not None:
            self.sems[inc2][1] += k2
            h2 = self.sems[inc2][0]

        def f(e, build=build, h=h, k=k, h2=h2, k2=k2):
            ins = build(e)
            if h is not None:
                ins = ins.then_inc(h, k)
            if h2 is not None:
                ins.then_inc(h2, k2)
        self.emit(eng, f)

    def flush(self, block):
        q = self.q

        @block.sync
        def _(e):
            for f in q["sync"]:
                f(e)

        @block.gpsimd
        def _(e):
            for f in q["gpsimd"]:
                f(e)

        @block.vector
        def _(e):
            for f in q["vector"]:
                f(e)

        @block.scalar
        def _(e):
            for f in q["scalar"]:
                f(e)

        @block.tensor
        def _(e):
            for f in q["tensor"]:
                f(e)
        self.q = {e: [] for e in self.ENGS}


def build_nc():
    nc = bass.Bass("TRN2", target_bir_lowering=False, detect_race_conditions=False)

    xin = nc.dram_tensor("xin", [D, TOK], F32, kind="ExternalInput").ap()
    waT = nc.dram_tensor("waT", [D, D], F32, kind="ExternalInput").ap()
    wxT = nc.dram_tensor("wxT", [D, D], F32, kind="ExternalInput").ap()
    bafm = nc.dram_tensor("bafm", [128, KD], F32, kind="ExternalInput").ap()
    bxfm = nc.dram_tensor("bxfm", [128, KD], F32, kind="ExternalInput").ap()
    lamfm = nc.dram_tensor("lamfm", [128, KD], F32, kind="ExternalInput").ap()
    lutin = nc.dram_tensor("lutin", [1, 16], F32, kind="ExternalInput").ap()
    selin = nc.dram_tensor("selin", [1, 8], F32, kind="ExternalInput").ap()
    gsh = nc.dram_tensor("gsh", [D, BCOL], I32, kind="ExternalInput").ap()
    ush = nc.dram_tensor("ush", [D, BCOL], I32, kind="ExternalInput").ap()
    dsh = nc.dram_tensor("dsh", [I5, 256], I32, kind="ExternalInput").ap()
    outfm = nc.dram_tensor("outfm", [D, TOK], F32, kind="ExternalOutput").ap()

    wsh_ga = nc.dram_tensor("wsh_ga", [1, GSZ // 3], BF16, kind="Internal").ap()
    wfl_ga = nc.dram_tensor("wfl_ga", [8, GSZ // 3], BF16, kind="Internal").ap()
    wsh_gb = nc.dram_tensor("wsh_gb", [1, 2 * GSZ // 3], BF16, kind="Internal").ap()
    wfl_gb = nc.dram_tensor("wfl_gb", [8, 2 * GSZ // 3], BF16, kind="Internal").ap()
    wsh_u = nc.dram_tensor("wsh_u", [1, GSZ], BF16, kind="Internal").ap()
    wfl_u = nc.dram_tensor("wfl_u", [8, GSZ], BF16, kind="Internal").ap()
    wsh_d = nc.dram_tensor("wsh_d", [1, DSZ], BF16, kind="Internal").ap()
    wfl_d = nc.dram_tensor("wfl_d", [8, DSZ], BF16, kind="Internal").ap()
    wabf = nc.dram_tensor("wabf", [128, 8 * KD * 256], BF16, kind="Internal").ap()
    wxbf = nc.dram_tensor("wxbf", [128, 8 * KD * 256], BF16, kind="Internal").ap()
    x1d = nc.dram_tensor("x1d", [KD, 128, TOK], F32, kind="Internal").ap()
    x2d = nc.dram_tensor("x2d", [KD, 128, TOK], F32, kind="Internal").ap()
    cd = nc.dram_tensor("cd", [KD, 128, TOK], BF16, kind="Internal").ap()
    cryl = nc.dram_tensor("cryl", [1, 128 * KD], F32, kind="Internal").ap()
    crya = nc.dram_tensor("crya", [8, 128 * KD], F32, kind="Internal").ap()

    def gview_a(r):
        return wfl_ga[r, 0:GSZ // 3].rearrange("(p j c) -> p j c", p=128, j=KD)

    def gview_b(r):
        return wfl_gb[r, 0:2 * GSZ // 3].rearrange(
            "(p u j c) -> p u j c", p=128, u=2, j=KD)

    def uview(r):
        return wfl_u[r, 0:GSZ].rearrange("(p u j c) -> p u j c", p=128, u=3, j=KD)

    def dview(r):
        return wfl_d[r, 0:DSZ].rearrange("(p h i c) -> p h i c", p=128, h=2, i=KI)

    P = Prog()

    with contextlib.ExitStack() as es:
        sb = nc.sbuf_tensor
        lutsb = es.enter_context(sb("lutsb", [1, 16], F32))
        selsb = es.enter_context(sb("selsb", [1, 8], F32))
        ones1 = es.enter_context(sb("ones1", [1, 128], F32))
        onesb = es.enter_context(sb("onesb", [128, 128], BF16))
        kcon = es.enter_context(sb("kcon", [128, 16], F32))
        lutpp = es.enter_context(sb("lutpp", [128, 16], F32))
        selpp = es.enter_context(sb("selpp", [128, 8], F32))
        bapp = es.enter_context(sb("bapp", [128, KD], F32))
        bxpp = es.enter_context(sb("bxpp", [128, KD], F32))
        lampp = es.enter_context(sb("lampp", [128, KD], F32))
        kap = es.enter_context(sb("kap", [128, KD], F32))
        kap2 = es.enter_context(sb("kap2", [128, KD], F32))
        spt = es.enter_context(sb("spt", [128, KD], F32))
        epspp = es.enter_context(sb("epspp", [128, 1], F32))
        sps = es.enter_context(sb("sps", [128, KD], F32))
        hend = es.enter_context(sb("hend", [128, KD], F32))
        hin = es.enter_context(sb("hin", [128, KD], F32))
        hall = es.enter_context(sb("hall", [128, 8, KD], F32))
        AF = es.enter_context(sb("AF", [128, 20480], F32))
        AH = es.enter_context(sb("AH", [128, 44032], BF16))
        AW = es.enter_context(sb("AW", [128, 16384], BF16))
        PS = es.enter_context(nc.psum_tensor("PS", [128, 4096], F32))

        AFb = AF.ap().bitcast(BF16)

        P.new_sem(nc, es,
                  "prm", "vini", "pebc", "scbc", "scsp", "vkap",
                  "igg", "igu", "igu2", "igd", "wc", "ws", "cc",
                  "xld", "xld2", "x1st", "x2st", "ost", "hld",
                  "wsA", "wsB",
                  "scq", "teq", "pest", "veln",
                  "sch", "veh", "ped", "ved",
                  "peri", "scri", "sca", "vet", "scsq", "veu", "vesc", "vehc",
                  "vecor")

        for dst, src in ((lutsb, lutin), (selsb, selin), (bapp, bafm),
                         (bxpp, bxfm), (lampp, lamfm)):
            P.op("sync", lambda e, d=dst, s=src: e.dma_start(out=d[:, :], in_=s[:, :]),
                 inc="prm", k=16)
        PRM_T = P.n("prm")

        for k in range(16):
            P.op("vector", lambda e, k=k: e.memset(kcon[:, k:k + 1], float(k)))
        P.op("vector", lambda e: e.memset(ones1[:, :], 1.0))
        P.op("vector", lambda e: e.memset(epspp[:, :], EPS))
        P.op("vector", lambda e: e.memset(onesb[:, :], 1.0), inc="vini")

        P.wait("tensor", "prm", PRM_T)
        P.wait("tensor", "vini", 1)
        P.op("tensor", lambda e: e.matmul(PS[:, 0:16], ones1[:, :], lutsb[:, :],
                                          start=True, stop=True), inc="pebc")
        P.op("tensor", lambda e: e.matmul(PS[:, 512:520], ones1[:, :], selsb[:, :],
                                          start=True, stop=True), inc="pebc")
        P.wait("scalar", "pebc", 2)
        P.op("scalar", lambda e: e.copy(lutpp[:, :], PS[:, 0:16]), inc="scbc")
        P.op("scalar", lambda e: e.copy(selpp[:, :], PS[:, 512:520]), inc="scbc")
        P.wait("scalar", "prm", PRM_T)
        P.op("scalar", lambda e: e.activation(out=spt[:, :], in_=lampp[:, :],
                                              func=AF_.Exp, scale=-1.0))
        P.op("scalar", lambda e: e.activation(out=sps[:, :], in_=spt[:, :],
                                              func=AF_.Ln, bias=1.0), inc="scsp")
        P.wait("vector", "scsp", 1)
        P.op("vector", lambda e: e.tensor_scalar(out=kap[:, :], in0=sps[:, :],
                                                 scalar1=-8.0, scalar2=None,
                                                 op0=ALU.mult))
        P.op("vector", lambda e: e.tensor_scalar(out=kap2[:, :], in0=sps[:, :],
                                                 scalar1=-16.0, scalar2=None,
                                                 op0=ALU.mult), inc="vkap")

        idx_g3 = AH[:, 0:KD * BCOL].rearrange("p (u j c) -> p u j c", u=3, j=KD)
        P.op("gpsimd", lambda e: e.dma_start(
            out=idx_g3, in_=gsh.rearrange("(j p) (u c) -> p u j c", p=128, u=3)),
            inc="igg", k=16)

        dec_g = AFb[:, 16384:16384 + KD * BCOL]
        msk = AW[:, 0:KD * BCOL]
        marks = {}

        def emit_decode(idx_ap, dec_ap, nfree, msk_ap, inc_last="vesc"):
            for k in range(16):
                if k == 0:
                    P.op("vector", lambda e, i=idx_ap, d=dec_ap: e.tensor_scalar(
                        out=d, in0=i, scalar1=kcon[:, 0:1], scalar2=lutpp[:, 0:1],
                        op0=ALU.is_equal, op1=ALU.mult))
                else:
                    P.op("vector", lambda e, i=idx_ap, k=k, n=nfree, m=msk_ap:
                         e.tensor_scalar(
                             out=m[:, 0:n], in0=i, scalar1=kcon[:, k:k + 1],
                             scalar2=lutpp[:, k:k + 1], op0=ALU.is_equal, op1=ALU.mult))
                    P.op("vector", lambda e, d=dec_ap, n=nfree, m=msk_ap:
                         e.tensor_tensor(out=d, in0=d, in1=m[:, 0:n], op=ALU.add),
                         inc=(inc_last if k == 15 else None))

        P.wait("vector", "scbc", 2)
        P.wait("vector", "igg", 16)
        TUG = KD * 256
        emit_decode(AH[:, 0:TUG], dec_g[:, 0:TUG], TUG, msk)
        P.wait("gpsimd", "vesc", P.n("vesc"))
        P.op("gpsimd", lambda e: e.dma_start(
            out=wsh_ga[0, :].rearrange("(p x) -> p x", p=128),
            in_=dec_g[:, 0:TUG]), inc="ws", k=16)
        P.wait("gpsimd", "ws", P.n("ws"))
        P.op("gpsimd", lambda e: e.collective_compute(
            "AllGather", ALU.bypass, replica_groups=[list(range(8))],
            ins=[wsh_ga.opt()], outs=[wfl_ga.opt()]), inc="cc")
        for uu in (1, 2):
            emit_decode(AH[:, uu * TUG:(uu + 1) * TUG],
                        dec_g[:, uu * TUG:(uu + 1) * TUG], TUG, msk)
            P.wait("gpsimd", "vesc", P.n("vesc"))
            P.op("gpsimd", lambda e, uu=uu: e.dma_start(
                out=wsh_gb[0, :].rearrange("(p x) -> p x", p=128)[
                    :, (uu - 1) * TUG:uu * TUG],
                in_=dec_g[:, uu * TUG:(uu + 1) * TUG]), inc="ws", k=16)
        P.wait("gpsimd", "ws", P.n("ws"))
        P.op("gpsimd", lambda e: e.collective_compute(
            "AllGather", ALU.bypass, replica_groups=[list(range(8))],
            ins=[wsh_gb.opt()], outs=[wfl_gb.opt()]), inc="cc")
        marks["ws_g"] = P.n("ws")

        wshd_pm = wsh_d[0, 0:DSZ].rearrange("(p x) -> p x", p=128)
        wshu_pm = wsh_u[0, 0:GSZ].rearrange("(p x) -> p x", p=128)
        QD = 2752
        HU = KD * BCOL // 2

        def emit_down_up_decode_hook():
            TU = KD * 256
            uview_in = ush.rearrange("(j p) (u c) -> p u j c", p=128, u=3)
            t_vesc = {}
            for tt in range(3):
                aw_sl = 8192 + (tt % 2) * TU
                dec_sl = (tt % 2) * TU
                usem = "igu" if tt % 2 == 0 else "igu2"
                if tt >= 2:
                    P.wait("gpsimd", "vesc", t_vesc[tt - 2])
                P.op("gpsimd", lambda e, sl=aw_sl, tt=tt: e.dma_start(
                    out=AW[:, sl:sl + TU], in_=uview_in[:, tt]), inc=usem, k=16)
                idx_t = AW[:, aw_sl:aw_sl + TU]
                dec_t = AFb[:, dec_sl:dec_sl + TU]
                msk_t = AFb[:, 2 * TU:3 * TU]
                P.wait("vector", usem, P.n(usem))
                if tt >= 2:
                    P.wait("vector", "ws", P.n("ws"))
                emit_decode(idx_t, dec_t, TU, msk_t)
                t_vesc[tt] = P.n("vesc")
                P.wait("gpsimd", "vesc", P.n("vesc"))
                P.op("gpsimd", lambda e, tt=tt, sl=dec_sl: e.dma_start(
                    out=wshu_pm[:, tt * TU:(tt + 1) * TU],
                    in_=AFb[:, sl:sl + TU]), inc="ws", k=16)
            P.wait("gpsimd", "ws", P.n("ws"))
            P.op("gpsimd", lambda e: e.collective_compute(
                "AllGather", ALU.bypass, replica_groups=[list(range(8))],
                ins=[wsh_u.opt()], outs=[wfl_u.opt()]), inc="cc")
            HD = KI * 128
            dview_in = dsh.rearrange("(i p) (h c) -> p h i c", p=128, h=2)
            for hh in range(2):
                P.wait("gpsimd", "vesc", P.n("vesc"))
                P.op("gpsimd", lambda e, hh=hh: e.dma_start(
                    out=AW[:, 8192:8192 + HD], in_=dview_in[:, hh]),
                    inc="igd", k=16)
                P.wait("vector", "igd", P.n("igd"))
                P.wait("vector", "ws", P.n("ws"))
                emit_decode(AW[:, 8192:8192 + HD], AFb[:, 0:HD], HD,
                            AFb[:, HD:2 * HD])
                P.wait("gpsimd", "vesc", P.n("vesc"))
                P.op("gpsimd", lambda e, hh=hh: e.dma_start(
                    out=wshd_pm[:, hh * HD:(hh + 1) * HD],
                    in_=AFb[:, 0:HD]), inc="ws", k=16)
            P.wait("gpsimd", "ws", P.n("ws"))
            P.op("gpsimd", lambda e: e.collective_compute(
                "AllGather", ALU.bypass, replica_groups=[list(range(8))],
                ins=[wsh_d.opt()], outs=[wfl_d.opt()]), inc="cc")
            marks["ws_dec"] = P.n("ws")
            P.wait("vector", "ws", P.n("ws"))
            for aa in range(8):
                P.op("gpsimd", lambda e, a=aa: e.dma_start(
                    out=wabf.rearrange("p (a j c) -> p a j c", a=8, j=KD)[:, a],
                    in_=waT.rearrange("(j p) (a c) -> p a j c", p=128, a=8)[:, a]),
                    inc="wc", k=16)
                P.op("gpsimd", lambda e, a=aa: e.dma_start(
                    out=wxbf.rearrange("p (a j c) -> p a j c", a=8, j=KD)[:, a],
                    in_=wxT.rearrange("(j p) (a c) -> p a j c", p=128, a=8)[:, a]),
                    inc="wc", k=16)
            marks["wc_t"] = P.n("wc")

        xg = AF[:, 0:8192].rearrange("p (j t) -> p j t", j=KD)
        lnx = AFb[:, 16384:32768].rearrange("p (j t) -> p j t", j=KD)
        stmp = AF[:, 16384:17408]
        mrep = AF[:, 17408:17920]
        msq = AF[:, 17920:18432]
        var = AF[:, 18432:18944]
        rstd = AF[:, 18944:19456]
        xbuf = AFb[:, 38912:40960].rearrange("p (b t) -> p b t", b=4)
        h_all = AH[:, 0:KI * 1024].rearrange("p (i t) -> p i t", i=KI)
        wg_sl = [AW[:, 0:4096].rearrange("p (j c) -> p j c", j=KD),
                 AW[:, 4096:8192].rearrange("p (j c) -> p j c", j=KD)]
        wu_sl = [AW[:, 8192:12288].rearrange("p (j c) -> p j c", j=KD),
                 AW[:, 12288:16384].rearrange("p (j c) -> p j c", j=KD)]
        wd_sl = [AW[:, 0:5504].rearrange("p (i c) -> p i c", i=KI),
                 AW[:, 5504:11008].rearrange("p (i c) -> p i c", i=KI)]

        def emit_correction(cgrp, half_gate=False):
            for j in range(KD):
                if half_gate and j == 0:
                    P.wait("vector", "xld", P.n("xld"))
                if half_gate and j == 8:
                    P.wait("vector", "xld2", P.n("xld2"))
                P.op("vector", lambda e, j=j, c=cgrp: e.scalar_tensor_tensor(
                    out=xg[:, j, :], in0=c[:, j, :], scalar=hin[:, j:j + 1],
                    in1=xg[:, j, :], op0=ALU.mult, op1=ALU.add),
                    inc=("vecor" if (j == KD - 1 or (half_gate and j == 7))
                         else None))

        def emit_group_ln(g2, half2=None):
            base_teq = P.n("teq")
            for j in range(KD):
                if j == 8 and half2:
                    for eng, sm, v in half2:
                        P.wait(eng, sm, v)
                if j >= 2:
                    P.wait("scalar", "teq", base_teq + 2 * (j - 1))
                P.op("scalar", lambda e, j=j: e.copy(
                    xbuf[:, j % 2, :], xg[:, j, :]), inc="scq")
                P.op("scalar", lambda e, j=j: e.activation(
                    out=xbuf[:, 2 + j % 2, :], in_=xg[:, j, :],
                    func=AF_.Square), inc="scq")
            P.wait("tensor", "vet")
            base_scq = P.n("scq") - 2 * KD
            for j in range(KD):
                P.wait("tensor", "scq", base_scq + 2 * (j + 1))
                P.op("tensor", lambda e, j=j: e.matmul(
                    PS[:, 6 * 512:7 * 512], onesb[:, :], xbuf[:, j % 2, :],
                    start=(j == 0), stop=(j == KD - 1)), inc="teq")
                P.op("tensor", lambda e, j=j: e.matmul(
                    PS[:, 7 * 512:8 * 512], onesb[:, :], xbuf[:, 2 + j % 2, :],
                    start=(j == 0), stop=(j == KD - 1)),
                    inc=("pest" if j == KD - 1 else "teq"))
            P.wait("vector", "pest", P.n("pest"))
            P.op("vector", lambda e: e.tensor_scalar(
                out=mrep[:, :], in0=PS[:, 6 * 512:7 * 512], scalar1=INVD,
                scalar2=None, op0=ALU.mult))
            P.op("vector", lambda e: e.tensor_tensor(
                out=msq[:, :], in0=mrep[:, :], in1=mrep[:, :], op=ALU.mult))
            P.op("vector", lambda e: e.scalar_tensor_tensor(
                out=var[:, :], in0=PS[:, 7 * 512:8 * 512], scalar=INVD,
                in1=msq[:, :], op0=ALU.mult, op1=ALU.subtract), inc="vet")
            P.wait("scalar", "vet", P.n("vet"))
            P.op("scalar", lambda e: e.activation(
                out=msq[:, :], in_=var[:, :], func=AF_.Ln, bias=epspp[:, 0:1]))
            P.op("scalar", lambda e: e.activation(
                out=rstd[:, :], in_=msq[:, :], func=AF_.Exp, scale=-0.5), inc="sca")
            P.wait("vector", "sca", P.n("sca"))
            for j in range(KD):
                P.op("vector", lambda e, j=j: e.tensor_tensor(
                    out=stmp[:, 0:512], in0=xg[:, j, :], in1=mrep[:, :],
                    op=ALU.subtract))
                P.op("vector", lambda e, j=j, g2=g2: e.tensor_tensor(
                    out=lnx[:, j, g2 * G:(g2 + 1) * G], in0=stmp[:, 0:512],
                    in1=rstd[:, :], op=ALU.mult),
                    inc=("veln" if j == KD - 1 else None))

        def emit_ffn(pass_id, hook_after_ln=None):
            is3 = (pass_id == 3)
            st_sem = "ost" if is3 else "x1st"
            for st in range(NST):
                for g2 in range(2):
                    g = 2 * st + g2
                    gs = g * G
                    P.wait("sync", "veln")
                    P.wait("sync", st_sem, P.n(st_sem))
                    if is3:
                        P.wait("sync", "vecor")
                        P.wait("sync", "x2st", P.n("x2st"))
                        cgrp = lnx[:, :, g2 * G:(g2 + 1) * G]
                        x2v = x2d.rearrange("j p t -> p j t")
                        cdv = cd.rearrange("j p t -> p j t")
                        for hh, s in ((0, "xld"), (1, "xld2")):
                            P.op("sync", lambda e, gs=gs, hh=hh: e.dma_start(
                                out=xg[:, 8 * hh:8 * hh + 8, :],
                                in_=x2v[:, 8 * hh:8 * hh + 8, gs:gs + G]),
                                inc=s, k=16)
                            P.op("sync", lambda e, gs=gs, hh=hh, c=cgrp:
                                 e.dma_start(
                                     out=c[:, 8 * hh:8 * hh + 8, :],
                                     in_=cdv[:, 8 * hh:8 * hh + 8, gs:gs + G]),
                                 inc=s, k=16)
                        base_vecor = P.n("vecor")
                        emit_correction(cgrp, half_gate=True)
                        P.wait("scalar", "vecor", base_vecor + 1)
                        half2 = [("scalar", "vecor", base_vecor + 2)]
                    else:
                        xiv = xin.rearrange("(j p) t -> p j t", p=128)
                        for hh, s in ((0, "xld"), (1, "xld2")):
                            P.op("sync", lambda e, gs=gs, hh=hh: e.dma_start(
                                out=xg[:, 8 * hh:8 * hh + 8, :],
                                in_=xiv[:, 8 * hh:8 * hh + 8, gs:gs + G]),
                                inc=s, k=16)
                        P.wait("scalar", "xld", P.n("xld"))
                        half2 = [("scalar", "xld2", P.n("xld2"))]
                    if not is3 and st == 0:
                        P.wait("vector", "ws", marks["ws_g"])
                    emit_group_ln(g2, half2)
                if hook_after_ln is not None and st == 0:
                    hook_after_ln()
                    hook_after_ln = None
                VLN = P.n("veln")

                P.wait("tensor", "veln", VLN)
                P.wait("tensor", "cc", 1)
                P.wait("tensor", "scbc", 2)
                PED0 = P.n("ped")
                PERI0 = P.n("peri")
                unit_ped = {}
                it_n = 0
                gate_units = [(uu, r) for uu in range(3) for r in range(8)
                              if 6 * r + 2 * uu <= 42]
                for u, (uu, r) in enumerate(gate_units):
                    b = u % 2
                    if u < 2:
                        P.wait("sync", "ped", PED0)
                        P.wait("sync", "peri", PERI0)
                    else:
                        P.wait("sync", "ped", unit_ped[u - 2])
                    if uu == 0 and r == 0:
                        P.wait("sync", "cc", 1)
                    if uu == 1 and r == 0:
                        P.wait("sync", "cc", 2)
                    sem = "wsA" if b == 0 else "wsB"
                    if uu == 0:
                        P.op("sync", lambda e, r=r, b=b: e.dma_start(
                            out=wg_sl[b][:, :, :], in_=gview_a(r)), inc=sem, k=16)
                    else:
                        P.op("sync", lambda e, r=r, uu=uu, b=b: e.dma_start(
                            out=wg_sl[b][:, :, :], in_=gview_b(r)[:, uu - 1]),
                            inc=sem, k=16)
                    val_u = P.n(sem)
                    for mi, m in enumerate((6 * r + 2 * uu, 6 * r + 2 * uu + 1)):
                        if m >= KI:
                            continue
                        col = mi * 128
                        pp = it_n % 2
                        it_n += 1
                        P.wait("tensor", sem, val_u)
                        if it_n > 2:
                            P.wait("tensor", "sch", P.n("sch") - 1)
                        for k in range(KD):
                            for g2 in range(2):
                                bk = 2 * pp + g2
                                last = (k == KD - 1 and g2 == 1)
                                P.op("tensor",
                                     lambda e, b=b, k=k, col=col, bk=bk,
                                     g2=g2: e.matmul(
                                         PS[:, bk * 512:(bk + 1) * 512],
                                         wg_sl[b][:, k, col:col + 128],
                                         lnx[:, k, g2 * G:(g2 + 1) * G],
                                         start=(k == 0), stop=(k == KD - 1)),
                                     inc=("ped" if last else None))
                        P.wait("scalar", "ped", P.n("ped"))
                        P.op("scalar", lambda e, m=m, pp=pp: e.activation(
                            out=h_all[:, m, :],
                            in_=PS[:, (2 * pp) * 512:(2 * pp + 2) * 512],
                            func=AF_.Silu), inc="sch")
                    unit_ped[u] = P.n("ped")

                P.wait("sync", "cc", 3 if not is3 else 0)
                unit_ped_u = {}
                for u in range(22):
                    b = u % 2
                    r, uu = u // 3, u % 3
                    if u < 2:
                        P.wait("sync", "ped", unit_ped[20 + u])
                        if u == 0 and not is3 and st == 0 and "ws_dec" in marks:
                            P.wait("sync", "ws", marks["ws_dec"])
                    else:
                        P.wait("sync", "ped", unit_ped_u[u - 2])
                    sem = "wsA" if b == 0 else "wsB"
                    P.op("sync", lambda e, r=r, uu=uu, b=b: e.dma_start(
                        out=wu_sl[b][:, :, :], in_=uview(r)[:, uu]), inc=sem, k=16)
                    val_u = P.n(sem)
                    for m in (2 * u, 2 * u + 1):
                        if m >= KI:
                            continue
                        col = (m % 2) * 128
                        pp = m % 2
                        P.wait("tensor", sem, val_u)
                        if m >= 2:
                            P.wait("tensor", "veh", P.n("veh") - 1)
                        elif u == 0:
                            P.wait("tensor", "sch", P.n("sch"))
                        for k in range(KD):
                            for g2 in range(2):
                                bk = 4 + 2 * pp + g2
                                last = (k == KD - 1 and g2 == 1)
                                P.op("tensor",
                                     lambda e, b=b, k=k, col=col, bk=bk,
                                     g2=g2: e.matmul(
                                         PS[:, bk * 512:(bk + 1) * 512],
                                         wu_sl[b][:, k, col:col + 128],
                                         lnx[:, k, g2 * G:(g2 + 1) * G],
                                         start=(k == 0), stop=(k == KD - 1)),
                                     inc=("ped" if last else None))
                        P.wait("vector", "ped", P.n("ped"))
                        P.op("vector", lambda e, m=m, pp=pp: e.tensor_tensor(
                            out=h_all[:, m, :], in0=h_all[:, m, :],
                            in1=PS[:, (4 + 2 * pp) * 512:(4 + 2 * pp + 2) * 512],
                            op=ALU.mult), inc="veh")
                    unit_ped_u[u] = P.n("ped")

                GU_VEH = P.n("veh")
                PED_DN = P.n("ped")
                round_ped = {}
                dn_round = 0
                for g2 in range(2):
                    g = 2 * st + g2
                    gs = g * G
                    P.wait("sync", "veln", VLN)
                    if not is3 and st == 0 and g2 == 0 and "ws_dec" in marks:
                        P.wait("sync", "ws", marks["ws_dec"])
                    if g2 == 1:
                        P.wait("sync", st_sem, P.n(st_sem))
                    if is3:
                        P.wait("sync", "vecor")
                        P.wait("sync", "ped", PED_DN)
                        cgrp = lnx[:, :, g2 * G:(g2 + 1) * G]
                        P.op("sync", lambda e, gs=gs: e.dma_start(
                            out=xg[:, :, :],
                            in_=x2d.rearrange("j p t -> p j t")[:, :, gs:gs + G]),
                            inc="xld", k=16)
                        P.op("sync", lambda e, gs=gs, c=cgrp: e.dma_start(
                            out=c, in_=cd.rearrange("j p t -> p j t")[:, :, gs:gs + G]),
                            inc="xld", k=16)
                        P.wait("vector", "xld", P.n("xld"))
                        emit_correction(cgrp)
                    else:
                        P.op("sync", lambda e, gs=gs: e.dma_start(
                            out=xg[:, :, :],
                            in_=xin.rearrange("(j p) t -> p j t", p=128)[:, :, gs:gs + G]),
                            inc="xld", k=16)
                        P.wait("vector", "xld", P.n("xld"))
                    for f in range(KD):
                        db = dn_round % 2
                        rb, ch = f // 2, f % 2
                        if dn_round < 2:
                            P.wait("sync", "ped", PED_DN)
                        else:
                            P.wait("sync", "ped", round_ped[dn_round - 2])
                        sem = "wsA" if db == 0 else "wsB"
                        if dn_round == 0 and st == 0 and pass_id == 1:
                            P.wait("sync", "cc", 4)
                        P.op("sync", lambda e, rb=rb, ch=ch, db=db: e.dma_start(
                            out=wd_sl[db][:, :, :],
                            in_=dview(rb)[:, ch]), inc=sem, k=16)
                        val_u = P.n(sem)
                        P.wait("tensor", sem, val_u)
                        if g2 == 0 and f == 0:
                            P.wait("tensor", "veh", GU_VEH)
                        slot = dn_round % 4
                        if dn_round >= 4:
                            P.wait("tensor", "ved", P.n("ved") - 3)
                        for i in range(KI):
                            P.op("tensor", lambda e, i=i, db=db, slot=slot,
                                 g2=g2: e.matmul(
                                     PS[:, slot * 512:(slot + 1) * 512],
                                     wd_sl[db][:, i, :],
                                     h_all[:, i, g2 * G:(g2 + 1) * G],
                                     start=(i == 0), stop=(i == KI - 1)),
                                 inc=("ped" if i == KI - 1 else None))
                        round_ped[dn_round] = P.n("ped")
                        P.wait("vector", "ped", P.n("ped"))
                        P.op("vector", lambda e, f=f, slot=slot: e.scalar_tensor_tensor(
                            out=xg[:, f, :], in0=PS[:, slot * 512:(slot + 1) * 512],
                            scalar=0.5, in1=xg[:, f, :], op0=ALU.mult, op1=ALU.add),
                            inc="ved")
                        P.wait("gpsimd", "ved", P.n("ved"))
                        if is3:
                            P.op("gpsimd", lambda e, gs=gs, f=f: e.dma_start(
                                out=outfm.rearrange("(j p) t -> p j t",
                                                    p=128)[:, f, gs:gs + G],
                                in_=xg[:, f, :]), inc="ost", k=16)
                        else:
                            P.op("gpsimd", lambda e, gs=gs, f=f: e.dma_start(
                                out=x1d.rearrange("j p t -> p j t")[:, f,
                                                                    gs:gs + G],
                                in_=xg[:, f, :]), inc="x1st", k=16)
                        dn_round += 1

        emit_ffn(1, hook_after_ln=emit_down_up_decode_hook)

        def emit_attn():
            slR = AF[:, 8192:10240]
            slI = AF[:, 10240:12288]
            slA = AF[:, 12288:14336]
            slQ = AF[:, 14336:16384]
            slX = AF[:, 16384:18432]
            slE = AF[:, 18432:20480]
            mrep2 = slR[:, 0:512]
            msq2 = slR[:, 512:1024]
            var2 = slR[:, 1024:1536]
            rstd2 = slR[:, 1536:2048]
            xbuf2 = AF[:, 10240:11264].bitcast(BF16).rearrange(
                "p (b t) -> p b t", b=4)
            lnx2 = AH[:, 0:KD * TOK].rearrange("p (j t) -> p j t", j=KD)
            cbuf = AH[:, KD * TOK:KD * TOK + TOK]

            xg2 = AF[:, 8192:16384].rearrange("p (j t) -> p j t", j=KD)
            mrep2 = AF[:, 16384:16896]
            msq2 = AF[:, 16896:17408]
            var2 = AF[:, 17408:17920]
            rstd2 = AF[:, 17920:18432]
            xbuf2 = AF[:, 18432:19456].bitcast(BF16).rearrange(
                "p (b t) -> p b t", b=4)
            for g in range(4):
                gs = g * G
                xgb = xg if g % 2 == 0 else xg2
                if g < 2:
                    P.wait("sync", "x1st", P.n("x1st"))
                else:
                    P.wait("sync", "veln", P.n("veln") - 1)
                P.op("sync", lambda e, gs=gs, xgb=xgb: e.dma_start(
                    out=xgb[:, :, :],
                    in_=x1d.rearrange("j p t -> p j t")[:, :, gs:gs + G]),
                    inc="xld", k=16)
                P.wait("scalar", "xld", P.n("xld"))
                base_teq = P.n("teq")
                for j in range(KD):
                    if j >= 2:
                        P.wait("scalar", "teq", base_teq + 2 * (j - 1))
                    P.op("scalar", lambda e, j=j, xgb=xgb: e.copy(
                        xbuf2[:, j % 2, :], xgb[:, j, :]), inc="scq")
                    P.op("scalar", lambda e, j=j, xgb=xgb: e.activation(
                        out=xbuf2[:, 2 + j % 2, :], in_=xgb[:, j, :],
                        func=AF_.Square), inc="scq")
                P.wait("tensor", "vet")
                base_scq = P.n("scq") - 2 * KD
                for j in range(KD):
                    P.wait("tensor", "scq", base_scq + 2 * (j + 1))
                    P.op("tensor", lambda e, j=j: e.matmul(
                        PS[:, 6 * 512:7 * 512], onesb[:, :], xbuf2[:, j % 2, :],
                        start=(j == 0), stop=(j == KD - 1)), inc="teq")
                    P.op("tensor", lambda e, j=j: e.matmul(
                        PS[:, 7 * 512:8 * 512], onesb[:, :], xbuf2[:, 2 + j % 2, :],
                        start=(j == 0), stop=(j == KD - 1)),
                        inc=("pest" if j == KD - 1 else "teq"))
                P.wait("vector", "pest", P.n("pest"))
                P.op("vector", lambda e: e.tensor_scalar(
                    out=mrep2, in0=PS[:, 6 * 512:7 * 512], scalar1=INVD,
                    scalar2=None, op0=ALU.mult))
                P.op("vector", lambda e: e.tensor_tensor(
                    out=msq2, in0=mrep2, in1=mrep2, op=ALU.mult))
                P.op("vector", lambda e: e.scalar_tensor_tensor(
                    out=var2, in0=PS[:, 7 * 512:8 * 512], scalar=INVD,
                    in1=msq2, op0=ALU.mult, op1=ALU.subtract), inc="vet")
                P.wait("scalar", "vet", P.n("vet"))
                P.op("scalar", lambda e: e.activation(
                    out=msq2, in_=var2, func=AF_.Ln, bias=epspp[:, 0:1]))
                P.op("scalar", lambda e: e.activation(
                    out=rstd2, in_=msq2, func=AF_.Exp, scale=-0.5), inc="sca")
                P.wait("vector", "sca", P.n("sca"))
                tmp2 = AF[:, 19456:19968]
                for j in range(KD):
                    P.op("vector", lambda e, j=j, xgb=xgb, t=tmp2: e.tensor_tensor(
                        out=t, in0=xgb[:, j, :], in1=mrep2,
                        op=ALU.subtract))
                    P.op("vector", lambda e, j=j, gs=gs, t=tmp2: e.tensor_tensor(
                        out=lnx2[:, j, gs:gs + G], in0=t,
                        in1=rstd2, op=ALU.mult),
                        inc=("veln" if j == KD - 1 else None))
            VLN2 = P.n("veln")
            P.op("vector", lambda e: e.memset(slE[:, :], 0.0))

            P.wait("sync", "veln", VLN2)
            P.wait("tensor", "veln", VLN2)
            P.wait("tensor", "ved", P.n("ved"))
            P.wait("scalar", "veln", VLN2)
            P.wait("scalar", "vkap", 1)
            PED_A = P.n("ped")
            PERI_A = P.n("peri")
            pair_peri = {}
            for mp in range(8):
                b = mp % 2
                if mp < 2:
                    P.wait("sync", "wc", marks["wc_t"])
                    P.wait("sync", "ped", PED_A)
                else:
                    P.wait("sync", "peri", pair_peri[mp - 2])
                sem = "wsA" if b == 0 else "wsB"
                P.op("sync", lambda e, mp=mp, b=b: e.dma_start(
                    out=wg_sl[b][:, :, :],
                    in_=wabf.rearrange("p (a j c) -> p a j c", a=8, j=KD)[:, mp]),
                    inc=sem, k=16)
                P.op("sync", lambda e, mp=mp, b=b: e.dma_start(
                    out=wu_sl[b][:, :, :],
                    in_=wxbf.rearrange("p (a j c) -> p a j c", a=8, j=KD)[:, mp]),
                    inc=sem, k=16)
                val_u = P.n(sem)
                for m in (2 * mp, 2 * mp + 1):
                    col = (m % 2) * 128
                    P.wait("tensor", sem, val_u)
                    if m >= 1:
                        P.wait("tensor", "scri", P.n("scri"))
                    for k in range(KD):
                        for mat, sl in ((0, wg_sl[b]), (1, wu_sl[b])):
                            for g in range(4):
                                last = (k == KD - 1 and mat == 1 and g == 3)
                                P.op("tensor", lambda e, sl=sl, k=k, col=col,
                                     mat=mat, g=g: e.matmul(
                                         PS[:, (4 * mat + g) * 512:
                                            (4 * mat + g + 1) * 512],
                                         sl[:, k, col:col + 128],
                                         lnx2[:, k, g * G:(g + 1) * G],
                                         start=(k == 0), stop=(k == KD - 1)),
                                     inc=("peri" if last else None))
                    P.wait("scalar", "peri", P.n("peri"))
                    if m >= 1:
                        P.wait("scalar", "veu", P.n("veu"))
                        P.wait("scalar", "vesc", P.n("vesc"))
                    P.op("scalar", lambda e, m=m: e.activation(
                        out=slR[:, :], in_=PS[:, 0:2048], func=AF_.Sigmoid,
                        bias=bapp[:, m:m + 1]))
                    P.op("scalar", lambda e, m=m: e.activation(
                        out=slI[:, :], in_=PS[:, 2048:4096], func=AF_.Sigmoid,
                        bias=bxpp[:, m:m + 1]), inc="scri")
                    P.op("scalar", lambda e, m=m: e.activation(
                        out=slA[:, :], in_=slR[:, :], func=AF_.Exp,
                        scale=kap[:, m:m + 1]))
                    P.op("scalar", lambda e, m=m: e.activation(
                        out=slQ[:, :], in_=slR[:, :], func=AF_.Exp,
                        scale=kap2[:, m:m + 1]), inc="sca")
                    P.wait("vector", "sca", P.n("sca"))
                    P.op("vector", lambda e: e.tensor_scalar(
                        out=slQ[:, :], in0=slQ[:, :], scalar1=-1.0, scalar2=1.0,
                        op0=ALU.mult, op1=ALU.add))
                    P.op("vector", lambda e: e.tensor_scalar(
                        out=slQ[:, :], in0=slQ[:, :], scalar1=0.0, scalar2=None,
                        op0=ALU.max), inc="vet")
                    P.wait("scalar", "vet", P.n("vet"))
                    P.op("scalar", lambda e: e.activation(
                        out=slR[:, :], in_=slQ[:, :], func=AF_.Sqrt), inc="scsq")
                    P.wait("vector", "scsq", P.n("scsq"))
                    P.op("vector", lambda e, m=m: e.tensor_tensor(
                        out=slI[:, :], in0=slI[:, :], in1=lnx2[:, m, :],
                        op=ALU.mult))
                    P.op("vector", lambda e: e.tensor_tensor(
                        out=slI[:, :], in0=slI[:, :], in1=slR[:, :], op=ALU.mult),
                        inc="veu")
                    P.op("vector", lambda e: e.tensor_tensor_scan(
                        out=slQ[:, :], data0=slA[:, :], data1=slI[:, :],
                        initial=0.0, op0=ALU.mult, op1=ALU.add), inc="vesc")
                    P.wait("vector", "vesc", P.n("vesc"))
                    P.op("vector", lambda e, m=m: e.tensor_copy(
                        hend[:, m:m + 1], slQ[:, 2047:2048]), inc="vehc")
                    P.wait("vector", "x2st", P.n("x2st"))
                    P.op("vector", lambda e: e.tensor_tensor_scan(
                        out=cbuf[:, :], data0=slA[:, :], data1=slE[:, :],
                        initial=1.0, op0=ALU.mult, op1=ALU.add), inc="vesc")
                    P.wait("sync", "x2st", P.n("x2st"))
                    P.op("sync", lambda e, m=m: e.dma_start(
                        out=slX[:, :], in_=x1d[m, :, :]), inc="xld", k=16)
                    P.wait("vector", "xld", P.n("xld"))
                    P.op("vector", lambda e: e.tensor_tensor(
                        out=slX[:, :], in0=slX[:, :], in1=slQ[:, :], op=ALU.add),
                        inc="vesc")
                    P.wait("gpsimd", "vesc", P.n("vesc"))
                    P.op("gpsimd", lambda e, m=m: e.dma_start(
                        out=x2d[m, :, :], in_=slX[:, :]), inc="x2st", k=16)
                    P.op("gpsimd", lambda e, m=m: e.dma_start(
                        out=cd[m, :, :], in_=cbuf[:, :]), inc="x2st", k=16)
                pair_peri[mp] = P.n("peri")

            P.wait("gpsimd", "vehc", P.n("vehc"))
            P.op("gpsimd", lambda e: e.dma_start(
                out=cryl[0, :].rearrange("(p j) -> p j", p=128), in_=hend[:, :]),
                inc="hld", k=16)
            P.wait("gpsimd", "hld", P.n("hld"))
            P.wait("gpsimd", "cc", 2)
            P.op("gpsimd", lambda e: e.collective_compute(
                "AllGather", ALU.bypass, replica_groups=[list(range(8))],
                ins=[cryl.opt()], outs=[crya.opt()]), inc="cc")
            P.wait("gpsimd", "cc", P.n("cc"))
            P.op("gpsimd", lambda e: e.dma_start(
                out=hall[:, :, :],
                in_=crya.rearrange("r (p j) -> p r j", p=128)), inc="hld", k=16)
            P.wait("vector", "hld", P.n("hld"))
            P.op("vector", lambda e: e.tensor_scalar(
                out=hin[:, :], in0=hall[:, 0, :], scalar1=selpp[:, 0:1],
                scalar2=None, op0=ALU.mult))
            for r in range(1, 8):
                P.op("vector", lambda e, r=r: e.scalar_tensor_tensor(
                    out=hin[:, :], in0=hall[:, r, :], scalar=selpp[:, r:r + 1],
                    in1=hin[:, :], op0=ALU.mult, op1=ALU.add),
                    inc=("vecor" if r == 7 else None))

        emit_attn()

        emit_ffn(3)

        P.wait("gpsimd", "ost", P.n("ost"))

        block = es.enter_context(nc.Block())
        P.flush(block)

    return nc


_NC_CACHE = None


def _get_nc():
    global _NC_CACHE
    if _NC_CACHE is None:
        _NC_CACHE = build_nc()
    return _NC_CACHE


def kernel(L0_attn_x, L0_attn_Wa, L0_attn_Wx, L0_attn_ba, L0_attn_bx,
           L0_attn_Lam, L0_ffn_lut, L0_ffn_gate_idx, L0_ffn_up_idx,
           L0_ffn_down_idx):
    x = np.asarray(L0_attn_x, dtype=np.float32)
    Wa = np.asarray(L0_attn_Wa, dtype=np.float32)
    Wx = np.asarray(L0_attn_Wx, dtype=np.float32)
    ba = np.asarray(L0_attn_ba, dtype=np.float32)
    bx = np.asarray(L0_attn_bx, dtype=np.float32)
    Lam = np.asarray(L0_attn_Lam, dtype=np.float32)
    lut = np.asarray(L0_ffn_lut, dtype=np.float32)
    gidx = np.asarray(L0_ffn_gate_idx, dtype=np.int32)
    uidx = np.asarray(L0_ffn_up_idx, dtype=np.int32)
    didx = np.asarray(L0_ffn_down_idx, dtype=np.int32)

    waTh = np.ascontiguousarray(Wa.T)
    wxTh = np.ascontiguousarray(Wx.T)

    def fm(v):
        return np.ascontiguousarray(v.reshape(KD, 128).T)

    bafm, bxfm, lamfm = fm(ba), fm(bx), fm(Lam)
    lutin = lut.reshape(1, 16)

    gT = np.ascontiguousarray(gidx.T)
    uT = np.ascontiguousarray(uidx.T)
    dT = np.ascontiguousarray(didx.T)
    gTp = np.pad(gT, ((0, 0), (0, IPADC * 128 - I5)))
    uTp = np.pad(uT, ((0, 0), (0, IPADC * 128 - I5)))

    in_maps = []
    for c in range(8):
        seq, half = c // 2, c % 2
        xs = x[seq, half * TOK:(half + 1) * TOK, :]
        sel = np.zeros((1, 8), np.float32)
        if half == 1:
            sel[0, c - 1] = 1.0
        in_maps.append(dict(
            xin=np.ascontiguousarray(xs.T),
            waT=waTh, wxT=wxTh, bafm=bafm, bxfm=bxfm, lamfm=lamfm,
            lutin=lutin, selin=sel,
            gsh=np.ascontiguousarray(gTp[:, BCOL * c:BCOL * (c + 1)]),
            ush=np.ascontiguousarray(uTp[:, BCOL * c:BCOL * (c + 1)]),
            dsh=np.ascontiguousarray(dT[:, 256 * c:256 * (c + 1)]),
        ))

    nc = _get_nc()
    res = run_bass_kernel_spmd(nc, in_maps, core_ids=list(range(8)))

    out = np.empty((4, 4096, 2048), np.float32)
    for c in range(8):
        seq, half = c // 2, c % 2
        out[seq, half * TOK:(half + 1) * TOK, :] = np.asarray(
            res.results[c]["outfm"]).T
    return out


# version 14 (final)
# speedup vs baseline: 1.0337x; 1.0028x over previous
import contextlib
import numpy as np

import concourse.bass as bass
import concourse.mybir as mybir
from concourse.bass_utils import run_bass_kernel_spmd

F32 = mybir.dt.float32
BF16 = mybir.dt.bfloat16
I32 = mybir.dt.int32
AF_ = mybir.ActivationFunctionType
ALU = mybir.AluOpType

D = 2048
I5 = 5504
KD = 16
KI = 43
TOK = 2048
G = 512
NST = 2
IPADC = 48
BCOL = 768
UCOL = 256
GSZ = 128 * KD * BCOL
DSZ = 128 * KI * 256
SHSZ = 2 * GSZ + DSZ
EPS = 1e-5
INVD = 1.0 / D


class Prog:
    ENGS = ("sync", "gpsimd", "vector", "scalar", "tensor")

    def __init__(self):
        self.q = {e: [] for e in self.ENGS}
        self.sems = {}

    def new_sem(self, nc, es, *names):
        for n in names:
            h = es.enter_context(nc.semaphore(n))
            self.sems[n] = [h, 0]

    def n(self, name):
        return self.sems[name][1]

    def emit(self, eng, fn):
        self.q[eng].append(fn)

    def wait(self, eng, name, val=None):
        v = self.sems[name][1] if val is None else val
        if v <= 0:
            return
        h = self.sems[name][0]
        self.emit(eng, lambda e, h=h, v=v: e.wait_ge(h, v))

    def op(self, eng, build, inc=None, k=1, inc2=None, k2=1):
        h = None
        if inc is not None:
            self.sems[inc][1] += k
            h = self.sems[inc][0]
        h2 = None
        if inc2 is not None:
            self.sems[inc2][1] += k2
            h2 = self.sems[inc2][0]

        def f(e, build=build, h=h, k=k, h2=h2, k2=k2):
            ins = build(e)
            if h is not None:
                ins = ins.then_inc(h, k)
            if h2 is not None:
                ins.then_inc(h2, k2)
        self.emit(eng, f)

    def flush(self, block):
        q = self.q

        @block.sync
        def _(e):
            for f in q["sync"]:
                f(e)

        @block.gpsimd
        def _(e):
            for f in q["gpsimd"]:
                f(e)

        @block.vector
        def _(e):
            for f in q["vector"]:
                f(e)

        @block.scalar
        def _(e):
            for f in q["scalar"]:
                f(e)

        @block.tensor
        def _(e):
            for f in q["tensor"]:
                f(e)
        self.q = {e: [] for e in self.ENGS}


def build_nc():
    nc = bass.Bass("TRN2", target_bir_lowering=False, detect_race_conditions=False)

    xin = nc.dram_tensor("xin", [D, TOK], F32, kind="ExternalInput").ap()
    waT = nc.dram_tensor("waT", [D, D], F32, kind="ExternalInput").ap()
    wxT = nc.dram_tensor("wxT", [D, D], F32, kind="ExternalInput").ap()
    bafm = nc.dram_tensor("bafm", [128, KD], F32, kind="ExternalInput").ap()
    bxfm = nc.dram_tensor("bxfm", [128, KD], F32, kind="ExternalInput").ap()
    lamfm = nc.dram_tensor("lamfm", [128, KD], F32, kind="ExternalInput").ap()
    lutin = nc.dram_tensor("lutin", [1, 16], F32, kind="ExternalInput").ap()
    selin = nc.dram_tensor("selin", [1, 8], F32, kind="ExternalInput").ap()
    gsh = nc.dram_tensor("gsh", [D, BCOL], I32, kind="ExternalInput").ap()
    ush = nc.dram_tensor("ush", [D, BCOL], I32, kind="ExternalInput").ap()
    dsh = nc.dram_tensor("dsh", [I5, 256], I32, kind="ExternalInput").ap()
    outfm = nc.dram_tensor("outfm", [D, TOK], F32, kind="ExternalOutput").ap()

    wsh_ga = nc.dram_tensor("wsh_ga", [1, GSZ // 3], BF16, kind="Internal").ap()
    wfl_ga = nc.dram_tensor("wfl_ga", [8, GSZ // 3], BF16, kind="Internal").ap()
    wsh_gb = nc.dram_tensor("wsh_gb", [1, 2 * GSZ // 3], BF16, kind="Internal").ap()
    wfl_gb = nc.dram_tensor("wfl_gb", [8, 2 * GSZ // 3], BF16, kind="Internal").ap()
    wsh_u = nc.dram_tensor("wsh_u", [1, GSZ], BF16, kind="Internal").ap()
    wfl_u = nc.dram_tensor("wfl_u", [8, GSZ], BF16, kind="Internal").ap()
    wsh_d = nc.dram_tensor("wsh_d", [1, DSZ], BF16, kind="Internal").ap()
    wfl_d = nc.dram_tensor("wfl_d", [8, DSZ], BF16, kind="Internal").ap()
    wabf = nc.dram_tensor("wabf", [128, 8 * KD * 256], BF16, kind="Internal").ap()
    wxbf = nc.dram_tensor("wxbf", [128, 8 * KD * 256], BF16, kind="Internal").ap()
    x1d = nc.dram_tensor("x1d", [KD, 128, TOK], F32, kind="Internal").ap()
    x2d = nc.dram_tensor("x2d", [KD, 128, TOK], F32, kind="Internal").ap()
    cd = nc.dram_tensor("cd", [KD, 128, TOK], BF16, kind="Internal").ap()
    cryl = nc.dram_tensor("cryl", [1, 128 * KD], F32, kind="Internal").ap()
    crya = nc.dram_tensor("crya", [8, 128 * KD], F32, kind="Internal").ap()

    def gview_a(r):
        return wfl_ga[r, 0:GSZ // 3].rearrange("(p j c) -> p j c", p=128, j=KD)

    def gview_b(r):
        return wfl_gb[r, 0:2 * GSZ // 3].rearrange(
            "(p u j c) -> p u j c", p=128, u=2, j=KD)

    def uview(r):
        return wfl_u[r, 0:GSZ].rearrange("(p u j c) -> p u j c", p=128, u=3, j=KD)

    def dview(r):
        return wfl_d[r, 0:DSZ].rearrange("(p h i c) -> p h i c", p=128, h=2, i=KI)

    P = Prog()

    with contextlib.ExitStack() as es:
        sb = nc.sbuf_tensor
        lutsb = es.enter_context(sb("lutsb", [1, 16], F32))
        selsb = es.enter_context(sb("selsb", [1, 8], F32))
        ones1 = es.enter_context(sb("ones1", [1, 128], F32))
        onesb = es.enter_context(sb("onesb", [128, 128], BF16))
        kcon = es.enter_context(sb("kcon", [128, 16], F32))
        lutpp = es.enter_context(sb("lutpp", [128, 16], F32))
        selpp = es.enter_context(sb("selpp", [128, 8], F32))
        bapp = es.enter_context(sb("bapp", [128, KD], F32))
        bxpp = es.enter_context(sb("bxpp", [128, KD], F32))
        lampp = es.enter_context(sb("lampp", [128, KD], F32))
        kap = es.enter_context(sb("kap", [128, KD], F32))
        kap2 = es.enter_context(sb("kap2", [128, KD], F32))
        spt = es.enter_context(sb("spt", [128, KD], F32))
        epspp = es.enter_context(sb("epspp", [128, 1], F32))
        sps = es.enter_context(sb("sps", [128, KD], F32))
        hend = es.enter_context(sb("hend", [128, KD], F32))
        hin = es.enter_context(sb("hin", [128, KD], F32))
        hall = es.enter_context(sb("hall", [128, 8, KD], F32))
        AF = es.enter_context(sb("AF", [128, 20480], F32))
        AH = es.enter_context(sb("AH", [128, 44032], BF16))
        AW = es.enter_context(sb("AW", [128, 16384], BF16))
        PS = es.enter_context(nc.psum_tensor("PS", [128, 4096], F32))

        AFb = AF.ap().bitcast(BF16)

        P.new_sem(nc, es,
                  "prm", "vini", "pebc", "scbc", "scsp", "vkap",
                  "igg", "igu", "igu2", "igd", "wc", "ws", "cc",
                  "xld", "xld2", "x1st", "x2st", "ost", "hld",
                  "wsA", "wsB",
                  "scq", "teq", "pest", "veln",
                  "sch", "veh", "ped", "ved",
                  "peri", "scri", "sca", "vet", "scsq", "veu", "vesc", "vehc",
                  "vecor")

        for dst, src in ((lutsb, lutin), (selsb, selin), (bapp, bafm),
                         (bxpp, bxfm), (lampp, lamfm)):
            P.op("sync", lambda e, d=dst, s=src: e.dma_start(out=d[:, :], in_=s[:, :]),
                 inc="prm", k=16)
        PRM_T = P.n("prm")

        for k in range(16):
            P.op("vector", lambda e, k=k: e.memset(kcon[:, k:k + 1], float(k)))
        P.op("vector", lambda e: e.memset(ones1[:, :], 1.0))
        P.op("vector", lambda e: e.memset(epspp[:, :], EPS))
        P.op("vector", lambda e: e.memset(onesb[:, :], 1.0), inc="vini")

        P.wait("tensor", "prm", PRM_T)
        P.wait("tensor", "vini", 1)
        P.op("tensor", lambda e: e.matmul(PS[:, 0:16], ones1[:, :], lutsb[:, :],
                                          start=True, stop=True), inc="pebc")
        P.op("tensor", lambda e: e.matmul(PS[:, 512:520], ones1[:, :], selsb[:, :],
                                          start=True, stop=True), inc="pebc")
        P.wait("scalar", "pebc", 2)
        P.op("scalar", lambda e: e.copy(lutpp[:, :], PS[:, 0:16]), inc="scbc")
        P.op("scalar", lambda e: e.copy(selpp[:, :], PS[:, 512:520]), inc="scbc")
        P.wait("scalar", "prm", PRM_T)
        P.op("scalar", lambda e: e.activation(out=spt[:, :], in_=lampp[:, :],
                                              func=AF_.Exp, scale=-1.0))
        P.op("scalar", lambda e: e.activation(out=sps[:, :], in_=spt[:, :],
                                              func=AF_.Ln, bias=1.0), inc="scsp")
        P.wait("vector", "scsp", 1)
        P.op("vector", lambda e: e.tensor_scalar(out=kap[:, :], in0=sps[:, :],
                                                 scalar1=-8.0, scalar2=None,
                                                 op0=ALU.mult))
        P.op("vector", lambda e: e.tensor_scalar(out=kap2[:, :], in0=sps[:, :],
                                                 scalar1=-16.0, scalar2=None,
                                                 op0=ALU.mult), inc="vkap")

        idx_g3 = AH[:, 0:KD * BCOL].rearrange("p (u j c) -> p u j c", u=3, j=KD)
        P.op("gpsimd", lambda e: e.dma_start(
            out=idx_g3, in_=gsh.rearrange("(j p) (u c) -> p u j c", p=128, u=3)),
            inc="igg", k=16)

        dec_g = AFb[:, 16384:16384 + KD * BCOL]
        msk = AW[:, 0:KD * BCOL]
        marks = {}

        def emit_decode(idx_ap, dec_ap, nfree, msk_ap, inc_last="vesc"):
            for k in range(16):
                if k == 0:
                    P.op("vector", lambda e, i=idx_ap, d=dec_ap: e.tensor_scalar(
                        out=d, in0=i, scalar1=kcon[:, 0:1], scalar2=lutpp[:, 0:1],
                        op0=ALU.is_equal, op1=ALU.mult))
                else:
                    P.op("vector", lambda e, i=idx_ap, k=k, n=nfree, m=msk_ap:
                         e.tensor_scalar(
                             out=m[:, 0:n], in0=i, scalar1=kcon[:, k:k + 1],
                             scalar2=lutpp[:, k:k + 1], op0=ALU.is_equal, op1=ALU.mult))
                    P.op("vector", lambda e, d=dec_ap, n=nfree, m=msk_ap:
                         e.tensor_tensor(out=d, in0=d, in1=m[:, 0:n], op=ALU.add),
                         inc=(inc_last if k == 15 else None))

        P.wait("vector", "scbc", 2)
        P.wait("vector", "igg", 16)
        TUG = KD * 256
        emit_decode(AH[:, 0:TUG], dec_g[:, 0:TUG], TUG, msk)
        P.wait("gpsimd", "vesc", P.n("vesc"))
        P.op("gpsimd", lambda e: e.dma_start(
            out=wsh_ga[0, :].rearrange("(p x) -> p x", p=128),
            in_=dec_g[:, 0:TUG]), inc="ws", k=16)
        P.wait("gpsimd", "ws", P.n("ws"))
        P.op("gpsimd", lambda e: e.collective_compute(
            "AllGather", ALU.bypass, replica_groups=[list(range(8))],
            ins=[wsh_ga.opt()], outs=[wfl_ga.opt()]), inc="cc")
        for uu in (1, 2):
            emit_decode(AH[:, uu * TUG:(uu + 1) * TUG],
                        dec_g[:, uu * TUG:(uu + 1) * TUG], TUG, msk)
            P.wait("gpsimd", "vesc", P.n("vesc"))
            P.op("gpsimd", lambda e, uu=uu: e.dma_start(
                out=wsh_gb[0, :].rearrange("(p x) -> p x", p=128)[
                    :, (uu - 1) * TUG:uu * TUG],
                in_=dec_g[:, uu * TUG:(uu + 1) * TUG]), inc="ws", k=16)
        P.wait("gpsimd", "ws", P.n("ws"))
        P.op("gpsimd", lambda e: e.collective_compute(
            "AllGather", ALU.bypass, replica_groups=[list(range(8))],
            ins=[wsh_gb.opt()], outs=[wfl_gb.opt()]), inc="cc")
        marks["ws_g"] = P.n("ws")

        wshd_pm = wsh_d[0, 0:DSZ].rearrange("(p x) -> p x", p=128)
        wshu_pm = wsh_u[0, 0:GSZ].rearrange("(p x) -> p x", p=128)
        QD = 2752
        HU = KD * BCOL // 2

        def emit_down_up_decode_hook():
            TU = KD * 256
            uview_in = ush.rearrange("(j p) (u c) -> p u j c", p=128, u=3)
            t_vesc = {}
            for tt in range(3):
                aw_sl = 8192 + (tt % 2) * TU
                dec_sl = (tt % 2) * TU
                usem = "igu" if tt % 2 == 0 else "igu2"
                if tt >= 2:
                    P.wait("gpsimd", "vesc", t_vesc[tt - 2])
                P.op("gpsimd", lambda e, sl=aw_sl, tt=tt: e.dma_start(
                    out=AW[:, sl:sl + TU], in_=uview_in[:, tt]), inc=usem, k=16)
                idx_t = AW[:, aw_sl:aw_sl + TU]
                dec_t = AFb[:, dec_sl:dec_sl + TU]
                msk_t = AFb[:, 2 * TU:3 * TU]
                P.wait("vector", usem, P.n(usem))
                if tt >= 2:
                    P.wait("vector", "ws", P.n("ws"))
                emit_decode(idx_t, dec_t, TU, msk_t)
                t_vesc[tt] = P.n("vesc")
                P.wait("gpsimd", "vesc", P.n("vesc"))
                P.op("gpsimd", lambda e, tt=tt, sl=dec_sl: e.dma_start(
                    out=wshu_pm[:, tt * TU:(tt + 1) * TU],
                    in_=AFb[:, sl:sl + TU]), inc="ws", k=16)
            P.wait("gpsimd", "ws", P.n("ws"))
            P.op("gpsimd", lambda e: e.collective_compute(
                "AllGather", ALU.bypass, replica_groups=[list(range(8))],
                ins=[wsh_u.opt()], outs=[wfl_u.opt()]), inc="cc")
            HD = KI * 128
            dview_in = dsh.rearrange("(i p) (h c) -> p h i c", p=128, h=2)
            for hh in range(2):
                P.wait("gpsimd", "vesc", P.n("vesc"))
                P.op("gpsimd", lambda e, hh=hh: e.dma_start(
                    out=AW[:, 8192:8192 + HD], in_=dview_in[:, hh]),
                    inc="igd", k=16)
                P.wait("vector", "igd", P.n("igd"))
                P.wait("vector", "ws", P.n("ws"))
                emit_decode(AW[:, 8192:8192 + HD], AFb[:, 0:HD], HD,
                            AFb[:, HD:2 * HD])
                P.wait("gpsimd", "vesc", P.n("vesc"))
                P.op("gpsimd", lambda e, hh=hh: e.dma_start(
                    out=wshd_pm[:, hh * HD:(hh + 1) * HD],
                    in_=AFb[:, 0:HD]), inc="ws", k=16)
            P.wait("gpsimd", "ws", P.n("ws"))
            P.op("gpsimd", lambda e: e.collective_compute(
                "AllGather", ALU.bypass, replica_groups=[list(range(8))],
                ins=[wsh_d.opt()], outs=[wfl_d.opt()]), inc="cc")
            marks["ws_dec"] = P.n("ws")
            P.wait("vector", "ws", P.n("ws"))
            for aa in range(8):
                P.op("gpsimd", lambda e, a=aa: e.dma_start(
                    out=wabf.rearrange("p (a j c) -> p a j c", a=8, j=KD)[:, a],
                    in_=waT.rearrange("(j p) (a c) -> p a j c", p=128, a=8)[:, a]),
                    inc="wc", k=16)
                P.op("gpsimd", lambda e, a=aa: e.dma_start(
                    out=wxbf.rearrange("p (a j c) -> p a j c", a=8, j=KD)[:, a],
                    in_=wxT.rearrange("(j p) (a c) -> p a j c", p=128, a=8)[:, a]),
                    inc="wc", k=16)
            marks["wc_t"] = P.n("wc")

        xg = AF[:, 0:8192].rearrange("p (j t) -> p j t", j=KD)
        lnx = AFb[:, 16384:32768].rearrange("p (j t) -> p j t", j=KD)
        stmp = AF[:, 16384:17408]
        mrep = AF[:, 17408:17920]
        msq = AF[:, 17920:18432]
        var = AF[:, 18432:18944]
        rstd = AF[:, 18944:19456]
        xbuf = AFb[:, 38912:40960].rearrange("p (b t) -> p b t", b=4)
        h_all = AH[:, 0:KI * 1024].rearrange("p (i t) -> p i t", i=KI)
        wg_sl = [AW[:, 0:4096].rearrange("p (j c) -> p j c", j=KD),
                 AW[:, 4096:8192].rearrange("p (j c) -> p j c", j=KD)]
        wu_sl = [AW[:, 8192:12288].rearrange("p (j c) -> p j c", j=KD),
                 AW[:, 12288:16384].rearrange("p (j c) -> p j c", j=KD)]
        wd_sl = [AW[:, 0:5504].rearrange("p (i c) -> p i c", i=KI),
                 AW[:, 5504:11008].rearrange("p (i c) -> p i c", i=KI)]

        def emit_correction(cgrp, half_gate=False):
            for j in range(KD):
                if half_gate and j == 0:
                    P.wait("vector", "xld", P.n("xld"))
                if half_gate and j == 8:
                    P.wait("vector", "xld2", P.n("xld2"))
                P.op("vector", lambda e, j=j, c=cgrp: e.scalar_tensor_tensor(
                    out=xg[:, j, :], in0=c[:, j, :], scalar=hin[:, j:j + 1],
                    in1=xg[:, j, :], op0=ALU.mult, op1=ALU.add),
                    inc=("vecor" if (j == KD - 1 or (half_gate and j == 7))
                         else None))

        def emit_group_ln(g2, half2=None):
            base_teq = P.n("teq")
            for j in range(KD):
                if j == 8 and half2:
                    for eng, sm, v in half2:
                        P.wait(eng, sm, v)
                if j >= 2:
                    P.wait("scalar", "teq", base_teq + 2 * (j - 1))
                P.op("scalar", lambda e, j=j: e.copy(
                    xbuf[:, j % 2, :], xg[:, j, :]), inc="scq")
                P.op("scalar", lambda e, j=j: e.activation(
                    out=xbuf[:, 2 + j % 2, :], in_=xg[:, j, :],
                    func=AF_.Square), inc="scq")
            P.wait("tensor", "vet")
            base_scq = P.n("scq") - 2 * KD
            for j in range(KD):
                P.wait("tensor", "scq", base_scq + 2 * (j + 1))
                P.op("tensor", lambda e, j=j: e.matmul(
                    PS[:, 6 * 512:7 * 512], onesb[:, :], xbuf[:, j % 2, :],
                    start=(j == 0), stop=(j == KD - 1)), inc="teq")
                P.op("tensor", lambda e, j=j: e.matmul(
                    PS[:, 7 * 512:8 * 512], onesb[:, :], xbuf[:, 2 + j % 2, :],
                    start=(j == 0), stop=(j == KD - 1)),
                    inc=("pest" if j == KD - 1 else "teq"))
            P.wait("vector", "pest", P.n("pest"))
            P.op("vector", lambda e: e.tensor_scalar(
                out=mrep[:, :], in0=PS[:, 6 * 512:7 * 512], scalar1=INVD,
                scalar2=None, op0=ALU.mult))
            P.op("vector", lambda e: e.tensor_tensor(
                out=msq[:, :], in0=mrep[:, :], in1=mrep[:, :], op=ALU.mult))
            P.op("vector", lambda e: e.scalar_tensor_tensor(
                out=var[:, :], in0=PS[:, 7 * 512:8 * 512], scalar=INVD,
                in1=msq[:, :], op0=ALU.mult, op1=ALU.subtract), inc="vet")
            P.wait("scalar", "vet", P.n("vet"))
            P.op("scalar", lambda e: e.activation(
                out=msq[:, :], in_=var[:, :], func=AF_.Ln, bias=epspp[:, 0:1]))
            P.op("scalar", lambda e: e.activation(
                out=rstd[:, :], in_=msq[:, :], func=AF_.Exp, scale=-0.5), inc="sca")
            P.wait("vector", "sca", P.n("sca"))
            for j in range(KD):
                P.op("vector", lambda e, j=j: e.tensor_tensor(
                    out=stmp[:, 0:512], in0=xg[:, j, :], in1=mrep[:, :],
                    op=ALU.subtract))
                P.op("vector", lambda e, j=j, g2=g2: e.tensor_tensor(
                    out=lnx[:, j, g2 * G:(g2 + 1) * G], in0=stmp[:, 0:512],
                    in1=rstd[:, :], op=ALU.mult),
                    inc=("veln" if j == KD - 1 else None))

        def emit_ffn(pass_id, hook_after_ln=None):
            is3 = (pass_id == 3)
            st_sem = "ost" if is3 else "x1st"
            for st in range(NST):
                for g2 in range(2):
                    g = 2 * st + g2
                    gs = g * G
                    P.wait("sync", "veln")
                    P.wait("sync", st_sem, P.n(st_sem))
                    if is3:
                        P.wait("sync", "vecor")
                        P.wait("sync", "x2st", P.n("x2st"))
                        cgrp = lnx[:, :, g2 * G:(g2 + 1) * G]
                        x2v = x2d.rearrange("j p t -> p j t")
                        cdv = cd.rearrange("j p t -> p j t")
                        for hh, s in ((0, "xld"), (1, "xld2")):
                            P.op("sync", lambda e, gs=gs, hh=hh: e.dma_start(
                                out=xg[:, 8 * hh:8 * hh + 8, :],
                                in_=x2v[:, 8 * hh:8 * hh + 8, gs:gs + G]),
                                inc=s, k=16)
                            P.op("sync", lambda e, gs=gs, hh=hh, c=cgrp:
                                 e.dma_start(
                                     out=c[:, 8 * hh:8 * hh + 8, :],
                                     in_=cdv[:, 8 * hh:8 * hh + 8, gs:gs + G]),
                                 inc=s, k=16)
                        base_vecor = P.n("vecor")
                        emit_correction(cgrp, half_gate=True)
                        P.wait("scalar", "vecor", base_vecor + 1)
                        half2 = [("scalar", "vecor", base_vecor + 2)]
                    else:
                        xiv = xin.rearrange("(j p) t -> p j t", p=128)
                        for hh, s in ((0, "xld"), (1, "xld2")):
                            P.op("sync", lambda e, gs=gs, hh=hh: e.dma_start(
                                out=xg[:, 8 * hh:8 * hh + 8, :],
                                in_=xiv[:, 8 * hh:8 * hh + 8, gs:gs + G]),
                                inc=s, k=16)
                        P.wait("scalar", "xld", P.n("xld"))
                        half2 = [("scalar", "xld2", P.n("xld2"))]
                    if not is3 and st == 0:
                        P.wait("vector", "ws", marks["ws_g"])
                    emit_group_ln(g2, half2)
                if hook_after_ln is not None and st == 0:
                    hook_after_ln()
                    hook_after_ln = None
                VLN = P.n("veln")

                P.wait("tensor", "veln", VLN)
                P.wait("tensor", "cc", 1)
                P.wait("tensor", "scbc", 2)
                PED0 = P.n("ped")
                PERI0 = P.n("peri")
                unit_ped = {}
                it_n = 0
                gate_units = [(uu, r) for uu in range(3) for r in range(8)
                              if 6 * r + 2 * uu <= 42]
                for u, (uu, r) in enumerate(gate_units):
                    b = u % 2
                    if u < 2:
                        P.wait("sync", "ped", PED0)
                        P.wait("sync", "peri", PERI0)
                    else:
                        P.wait("sync", "ped", unit_ped[u - 2])
                    if uu == 0 and r == 0:
                        P.wait("sync", "cc", 1)
                    if uu == 1 and r == 0:
                        P.wait("sync", "cc", 2)
                    sem = "wsA" if b == 0 else "wsB"
                    if uu == 0:
                        P.op("sync", lambda e, r=r, b=b: e.dma_start(
                            out=wg_sl[b][:, :, :], in_=gview_a(r)), inc=sem, k=16)
                    else:
                        P.op("sync", lambda e, r=r, uu=uu, b=b: e.dma_start(
                            out=wg_sl[b][:, :, :], in_=gview_b(r)[:, uu - 1]),
                            inc=sem, k=16)
                    val_u = P.n(sem)
                    for mi, m in enumerate((6 * r + 2 * uu, 6 * r + 2 * uu + 1)):
                        if m >= KI:
                            continue
                        col = mi * 128
                        pp = it_n % 2
                        it_n += 1
                        P.wait("tensor", sem, val_u)
                        if it_n > 2:
                            P.wait("tensor", "sch", P.n("sch") - 1)
                        for k in range(KD):
                            for g2 in range(2):
                                bk = 2 * pp + g2
                                last = (k == KD - 1 and g2 == 1)
                                P.op("tensor",
                                     lambda e, b=b, k=k, col=col, bk=bk,
                                     g2=g2: e.matmul(
                                         PS[:, bk * 512:(bk + 1) * 512],
                                         wg_sl[b][:, k, col:col + 128],
                                         lnx[:, k, g2 * G:(g2 + 1) * G],
                                         start=(k == 0), stop=(k == KD - 1)),
                                     inc=("ped" if last else None))
                        P.wait("scalar", "ped", P.n("ped"))
                        P.op("scalar", lambda e, m=m, pp=pp: e.activation(
                            out=h_all[:, m, :],
                            in_=PS[:, (2 * pp) * 512:(2 * pp + 2) * 512],
                            func=AF_.Silu), inc="sch")
                    unit_ped[u] = P.n("ped")

                P.wait("sync", "cc", 3 if not is3 else 0)
                unit_ped_u = {}
                for u in range(22):
                    b = u % 2
                    r, uu = u // 3, u % 3
                    if u < 2:
                        P.wait("sync", "ped", unit_ped[20 + u])
                        if u == 0 and not is3 and st == 0 and "ws_dec" in marks:
                            P.wait("sync", "ws", marks["ws_dec"])
                    else:
                        P.wait("sync", "ped", unit_ped_u[u - 2])
                    sem = "wsA" if b == 0 else "wsB"
                    P.op("sync", lambda e, r=r, uu=uu, b=b: e.dma_start(
                        out=wu_sl[b][:, :, :], in_=uview(r)[:, uu]), inc=sem, k=16)
                    val_u = P.n(sem)
                    for m in (2 * u, 2 * u + 1):
                        if m >= KI:
                            continue
                        col = (m % 2) * 128
                        pp = m % 2
                        P.wait("tensor", sem, val_u)
                        if m >= 2:
                            P.wait("tensor", "veh", P.n("veh") - 1)
                        elif u == 0:
                            P.wait("tensor", "sch", P.n("sch"))
                        for k in range(KD):
                            for g2 in range(2):
                                bk = 4 + 2 * pp + g2
                                last = (k == KD - 1 and g2 == 1)
                                P.op("tensor",
                                     lambda e, b=b, k=k, col=col, bk=bk,
                                     g2=g2: e.matmul(
                                         PS[:, bk * 512:(bk + 1) * 512],
                                         wu_sl[b][:, k, col:col + 128],
                                         lnx[:, k, g2 * G:(g2 + 1) * G],
                                         start=(k == 0), stop=(k == KD - 1)),
                                     inc=("ped" if last else None))
                        P.wait("vector", "ped", P.n("ped"))
                        P.op("vector", lambda e, m=m, pp=pp: e.tensor_tensor(
                            out=h_all[:, m, :], in0=h_all[:, m, :],
                            in1=PS[:, (4 + 2 * pp) * 512:(4 + 2 * pp + 2) * 512],
                            op=ALU.mult), inc="veh")
                    unit_ped_u[u] = P.n("ped")

                GU_VEH = P.n("veh")
                PED_DN = P.n("ped")
                round_ped = {}
                dn_round = 0
                for g2 in range(2):
                    g = 2 * st + g2
                    gs = g * G
                    P.wait("sync", "veln", VLN)
                    if not is3 and st == 0 and g2 == 0 and "ws_dec" in marks:
                        P.wait("sync", "ws", marks["ws_dec"])
                    if g2 == 1:
                        P.wait("sync", st_sem, P.n(st_sem))
                    if is3:
                        P.wait("sync", "vecor")
                        P.wait("sync", "ped", PED_DN)
                        cgrp = lnx[:, :, g2 * G:(g2 + 1) * G]
                        x2v = x2d.rearrange("j p t -> p j t")
                        cdv = cd.rearrange("j p t -> p j t")
                        for hh, s in ((0, "xld"), (1, "xld2")):
                            P.op("sync", lambda e, gs=gs, hh=hh: e.dma_start(
                                out=xg[:, 8 * hh:8 * hh + 8, :],
                                in_=x2v[:, 8 * hh:8 * hh + 8, gs:gs + G]),
                                inc=s, k=16)
                            P.op("sync", lambda e, gs=gs, hh=hh, c=cgrp:
                                 e.dma_start(
                                     out=c[:, 8 * hh:8 * hh + 8, :],
                                     in_=cdv[:, 8 * hh:8 * hh + 8, gs:gs + G]),
                                 inc=s, k=16)
                        emit_correction(cgrp, half_gate=True)
                    else:
                        xiv = xin.rearrange("(j p) t -> p j t", p=128)
                        for hh, s in ((0, "xld"), (1, "xld2")):
                            P.op("sync", lambda e, gs=gs, hh=hh: e.dma_start(
                                out=xg[:, 8 * hh:8 * hh + 8, :],
                                in_=xiv[:, 8 * hh:8 * hh + 8, gs:gs + G]),
                                inc=s, k=16)
                    for f in range(KD):
                        db = dn_round % 2
                        rb, ch = f // 2, f % 2
                        if dn_round < 2:
                            P.wait("sync", "ped", PED_DN)
                        else:
                            P.wait("sync", "ped", round_ped[dn_round - 2])
                        sem = "wsA" if db == 0 else "wsB"
                        if dn_round == 0 and st == 0 and pass_id == 1:
                            P.wait("sync", "cc", 4)
                        P.op("sync", lambda e, rb=rb, ch=ch, db=db: e.dma_start(
                            out=wd_sl[db][:, :, :],
                            in_=dview(rb)[:, ch]), inc=sem, k=16)
                        val_u = P.n(sem)
                        P.wait("tensor", sem, val_u)
                        if g2 == 0 and f == 0:
                            P.wait("tensor", "veh", GU_VEH)
                        slot = dn_round % 4
                        if dn_round >= 4:
                            P.wait("tensor", "ved", P.n("ved") - 3)
                        for i in range(KI):
                            P.op("tensor", lambda e, i=i, db=db, slot=slot,
                                 g2=g2: e.matmul(
                                     PS[:, slot * 512:(slot + 1) * 512],
                                     wd_sl[db][:, i, :],
                                     h_all[:, i, g2 * G:(g2 + 1) * G],
                                     start=(i == 0), stop=(i == KI - 1)),
                                 inc=("ped" if i == KI - 1 else None))
                        round_ped[dn_round] = P.n("ped")
                        if not is3:
                            if f == 0:
                                P.wait("vector", "xld", P.n("xld"))
                            elif f == 8:
                                P.wait("vector", "xld2", P.n("xld2"))
                        P.wait("vector", "ped", P.n("ped"))
                        P.op("vector", lambda e, f=f, slot=slot: e.scalar_tensor_tensor(
                            out=xg[:, f, :], in0=PS[:, slot * 512:(slot + 1) * 512],
                            scalar=0.5, in1=xg[:, f, :], op0=ALU.mult, op1=ALU.add),
                            inc="ved")
                        P.wait("gpsimd", "ved", P.n("ved"))
                        if is3:
                            P.op("gpsimd", lambda e, gs=gs, f=f: e.dma_start(
                                out=outfm.rearrange("(j p) t -> p j t",
                                                    p=128)[:, f, gs:gs + G],
                                in_=xg[:, f, :]), inc="ost", k=16)
                        else:
                            P.op("gpsimd", lambda e, gs=gs, f=f: e.dma_start(
                                out=x1d.rearrange("j p t -> p j t")[:, f,
                                                                    gs:gs + G],
                                in_=xg[:, f, :]), inc="x1st", k=16)
                        dn_round += 1

        emit_ffn(1, hook_after_ln=emit_down_up_decode_hook)

        def emit_attn():
            slR = AF[:, 8192:10240]
            slI = AF[:, 10240:12288]
            slA = AF[:, 12288:14336]
            slQ = AF[:, 14336:16384]
            slX = AF[:, 16384:18432]
            slE = AF[:, 18432:20480]
            mrep2 = slR[:, 0:512]
            msq2 = slR[:, 512:1024]
            var2 = slR[:, 1024:1536]
            rstd2 = slR[:, 1536:2048]
            xbuf2 = AF[:, 10240:11264].bitcast(BF16).rearrange(
                "p (b t) -> p b t", b=4)
            lnx2 = AH[:, 0:KD * TOK].rearrange("p (j t) -> p j t", j=KD)
            cbuf = AH[:, KD * TOK:KD * TOK + TOK]

            xg2 = AF[:, 8192:16384].rearrange("p (j t) -> p j t", j=KD)
            mrep2 = AF[:, 16384:16896]
            msq2 = AF[:, 16896:17408]
            var2 = AF[:, 17408:17920]
            rstd2 = AF[:, 17920:18432]
            xbuf2 = AF[:, 18432:19456].bitcast(BF16).rearrange(
                "p (b t) -> p b t", b=4)
            for g in range(4):
                gs = g * G
                xgb = xg if g % 2 == 0 else xg2
                if g < 2:
                    P.wait("sync", "x1st", P.n("x1st"))
                else:
                    P.wait("sync", "veln", P.n("veln") - 1)
                P.op("sync", lambda e, gs=gs, xgb=xgb: e.dma_start(
                    out=xgb[:, :, :],
                    in_=x1d.rearrange("j p t -> p j t")[:, :, gs:gs + G]),
                    inc="xld", k=16)
                P.wait("scalar", "xld", P.n("xld"))
                base_teq = P.n("teq")
                for j in range(KD):
                    if j >= 2:
                        P.wait("scalar", "teq", base_teq + 2 * (j - 1))
                    P.op("scalar", lambda e, j=j, xgb=xgb: e.copy(
                        xbuf2[:, j % 2, :], xgb[:, j, :]), inc="scq")
                    P.op("scalar", lambda e, j=j, xgb=xgb: e.activation(
                        out=xbuf2[:, 2 + j % 2, :], in_=xgb[:, j, :],
                        func=AF_.Square), inc="scq")
                P.wait("tensor", "vet")
                base_scq = P.n("scq") - 2 * KD
                for j in range(KD):
                    P.wait("tensor", "scq", base_scq + 2 * (j + 1))
                    P.op("tensor", lambda e, j=j: e.matmul(
                        PS[:, 6 * 512:7 * 512], onesb[:, :], xbuf2[:, j % 2, :],
                        start=(j == 0), stop=(j == KD - 1)), inc="teq")
                    P.op("tensor", lambda e, j=j: e.matmul(
                        PS[:, 7 * 512:8 * 512], onesb[:, :], xbuf2[:, 2 + j % 2, :],
                        start=(j == 0), stop=(j == KD - 1)),
                        inc=("pest" if j == KD - 1 else "teq"))
                P.wait("vector", "pest", P.n("pest"))
                P.op("vector", lambda e: e.tensor_scalar(
                    out=mrep2, in0=PS[:, 6 * 512:7 * 512], scalar1=INVD,
                    scalar2=None, op0=ALU.mult))
                P.op("vector", lambda e: e.tensor_tensor(
                    out=msq2, in0=mrep2, in1=mrep2, op=ALU.mult))
                P.op("vector", lambda e: e.scalar_tensor_tensor(
                    out=var2, in0=PS[:, 7 * 512:8 * 512], scalar=INVD,
                    in1=msq2, op0=ALU.mult, op1=ALU.subtract), inc="vet")
                P.wait("scalar", "vet", P.n("vet"))
                P.op("scalar", lambda e: e.activation(
                    out=msq2, in_=var2, func=AF_.Ln, bias=epspp[:, 0:1]))
                P.op("scalar", lambda e: e.activation(
                    out=rstd2, in_=msq2, func=AF_.Exp, scale=-0.5), inc="sca")
                P.wait("vector", "sca", P.n("sca"))
                tmp2 = AF[:, 19456:19968]
                for j in range(KD):
                    P.op("vector", lambda e, j=j, xgb=xgb, t=tmp2: e.tensor_tensor(
                        out=t, in0=xgb[:, j, :], in1=mrep2,
                        op=ALU.subtract))
                    P.op("vector", lambda e, j=j, gs=gs, t=tmp2: e.tensor_tensor(
                        out=lnx2[:, j, gs:gs + G], in0=t,
                        in1=rstd2, op=ALU.mult),
                        inc=("veln" if j == KD - 1 else None))
            VLN2 = P.n("veln")
            P.op("vector", lambda e: e.memset(slE[:, :], 0.0))

            P.wait("sync", "veln", VLN2)
            P.wait("tensor", "veln", VLN2)
            P.wait("tensor", "ved", P.n("ved"))
            P.wait("scalar", "veln", VLN2)
            P.wait("scalar", "vkap", 1)
            PED_A = P.n("ped")
            PERI_A = P.n("peri")
            pair_peri = {}
            for mp in range(8):
                b = mp % 2
                if mp < 2:
                    P.wait("sync", "wc", marks["wc_t"])
                    P.wait("sync", "ped", PED_A)
                else:
                    P.wait("sync", "peri", pair_peri[mp - 2])
                sem = "wsA" if b == 0 else "wsB"
                P.op("sync", lambda e, mp=mp, b=b: e.dma_start(
                    out=wg_sl[b][:, :, :],
                    in_=wabf.rearrange("p (a j c) -> p a j c", a=8, j=KD)[:, mp]),
                    inc=sem, k=16)
                P.op("sync", lambda e, mp=mp, b=b: e.dma_start(
                    out=wu_sl[b][:, :, :],
                    in_=wxbf.rearrange("p (a j c) -> p a j c", a=8, j=KD)[:, mp]),
                    inc=sem, k=16)
                val_u = P.n(sem)
                for m in (2 * mp, 2 * mp + 1):
                    col = (m % 2) * 128
                    P.wait("tensor", sem, val_u)
                    if m >= 1:
                        P.wait("tensor", "scri", P.n("scri"))
                    for k in range(KD):
                        for mat, sl in ((0, wg_sl[b]), (1, wu_sl[b])):
                            for g in range(4):
                                last = (k == KD - 1 and mat == 1 and g == 3)
                                P.op("tensor", lambda e, sl=sl, k=k, col=col,
                                     mat=mat, g=g: e.matmul(
                                         PS[:, (4 * mat + g) * 512:
                                            (4 * mat + g + 1) * 512],
                                         sl[:, k, col:col + 128],
                                         lnx2[:, k, g * G:(g + 1) * G],
                                         start=(k == 0), stop=(k == KD - 1)),
                                     inc=("peri" if last else None))
                    P.wait("scalar", "peri", P.n("peri"))
                    if m >= 1:
                        P.wait("scalar", "veu", P.n("veu"))
                        P.wait("scalar", "vesc", P.n("vesc"))
                    P.op("scalar", lambda e, m=m: e.activation(
                        out=slR[:, :], in_=PS[:, 0:2048], func=AF_.Sigmoid,
                        bias=bapp[:, m:m + 1]))
                    P.op("scalar", lambda e, m=m: e.activation(
                        out=slI[:, :], in_=PS[:, 2048:4096], func=AF_.Sigmoid,
                        bias=bxpp[:, m:m + 1]), inc="scri")
                    P.op("scalar", lambda e, m=m: e.activation(
                        out=slA[:, :], in_=slR[:, :], func=AF_.Exp,
                        scale=kap[:, m:m + 1]))
                    P.op("scalar", lambda e, m=m: e.activation(
                        out=slQ[:, :], in_=slR[:, :], func=AF_.Exp,
                        scale=kap2[:, m:m + 1]), inc="sca")
                    P.wait("vector", "sca", P.n("sca"))
                    P.op("vector", lambda e: e.tensor_scalar(
                        out=slQ[:, :], in0=slQ[:, :], scalar1=-1.0, scalar2=1.0,
                        op0=ALU.mult, op1=ALU.add))
                    P.op("vector", lambda e: e.tensor_scalar(
                        out=slQ[:, :], in0=slQ[:, :], scalar1=0.0, scalar2=None,
                        op0=ALU.max), inc="vet")
                    P.wait("scalar", "vet", P.n("vet"))
                    P.op("scalar", lambda e: e.activation(
                        out=slR[:, :], in_=slQ[:, :], func=AF_.Sqrt), inc="scsq")
                    P.wait("vector", "scsq", P.n("scsq"))
                    P.op("vector", lambda e, m=m: e.tensor_tensor(
                        out=slI[:, :], in0=slI[:, :], in1=lnx2[:, m, :],
                        op=ALU.mult))
                    P.op("vector", lambda e: e.tensor_tensor(
                        out=slI[:, :], in0=slI[:, :], in1=slR[:, :], op=ALU.mult),
                        inc="veu")
                    P.op("vector", lambda e: e.tensor_tensor_scan(
                        out=slQ[:, :], data0=slA[:, :], data1=slI[:, :],
                        initial=0.0, op0=ALU.mult, op1=ALU.add), inc="vesc")
                    P.wait("vector", "vesc", P.n("vesc"))
                    P.op("vector", lambda e, m=m: e.tensor_copy(
                        hend[:, m:m + 1], slQ[:, 2047:2048]), inc="vehc")
                    P.wait("vector", "x2st", P.n("x2st"))
                    P.op("vector", lambda e: e.tensor_tensor_scan(
                        out=cbuf[:, :], data0=slA[:, :], data1=slE[:, :],
                        initial=1.0, op0=ALU.mult, op1=ALU.add), inc="vesc")
                    P.wait("sync", "x2st", P.n("x2st"))
                    P.op("sync", lambda e, m=m: e.dma_start(
                        out=slX[:, :], in_=x1d[m, :, :]), inc="xld", k=16)
                    P.wait("vector", "xld", P.n("xld"))
                    P.op("vector", lambda e: e.tensor_tensor(
                        out=slX[:, :], in0=slX[:, :], in1=slQ[:, :], op=ALU.add),
                        inc="vesc")
                    P.wait("gpsimd", "vesc", P.n("vesc"))
                    P.op("gpsimd", lambda e, m=m: e.dma_start(
                        out=x2d[m, :, :], in_=slX[:, :]), inc="x2st", k=16)
                    P.op("gpsimd", lambda e, m=m: e.dma_start(
                        out=cd[m, :, :], in_=cbuf[:, :]), inc="x2st", k=16)
                pair_peri[mp] = P.n("peri")

            P.wait("gpsimd", "vehc", P.n("vehc"))
            P.op("gpsimd", lambda e: e.dma_start(
                out=cryl[0, :].rearrange("(p j) -> p j", p=128), in_=hend[:, :]),
                inc="hld", k=16)
            P.wait("gpsimd", "hld", P.n("hld"))
            P.wait("gpsimd", "cc", 2)
            P.op("gpsimd", lambda e: e.collective_compute(
                "AllGather", ALU.bypass, replica_groups=[list(range(8))],
                ins=[cryl.opt()], outs=[crya.opt()]), inc="cc")
            P.wait("gpsimd", "cc", P.n("cc"))
            P.op("gpsimd", lambda e: e.dma_start(
                out=hall[:, :, :],
                in_=crya.rearrange("r (p j) -> p r j", p=128)), inc="hld", k=16)
            P.wait("vector", "hld", P.n("hld"))
            P.op("vector", lambda e: e.tensor_scalar(
                out=hin[:, :], in0=hall[:, 0, :], scalar1=selpp[:, 0:1],
                scalar2=None, op0=ALU.mult))
            for r in range(1, 8):
                P.op("vector", lambda e, r=r: e.scalar_tensor_tensor(
                    out=hin[:, :], in0=hall[:, r, :], scalar=selpp[:, r:r + 1],
                    in1=hin[:, :], op0=ALU.mult, op1=ALU.add),
                    inc=("vecor" if r == 7 else None))

        emit_attn()

        emit_ffn(3)

        P.wait("gpsimd", "ost", P.n("ost"))

        block = es.enter_context(nc.Block())
        P.flush(block)

    return nc


_NC_CACHE = None


def _get_nc():
    global _NC_CACHE
    if _NC_CACHE is None:
        _NC_CACHE = build_nc()
    return _NC_CACHE


def kernel(L0_attn_x, L0_attn_Wa, L0_attn_Wx, L0_attn_ba, L0_attn_bx,
           L0_attn_Lam, L0_ffn_lut, L0_ffn_gate_idx, L0_ffn_up_idx,
           L0_ffn_down_idx):
    x = np.asarray(L0_attn_x, dtype=np.float32)
    Wa = np.asarray(L0_attn_Wa, dtype=np.float32)
    Wx = np.asarray(L0_attn_Wx, dtype=np.float32)
    ba = np.asarray(L0_attn_ba, dtype=np.float32)
    bx = np.asarray(L0_attn_bx, dtype=np.float32)
    Lam = np.asarray(L0_attn_Lam, dtype=np.float32)
    lut = np.asarray(L0_ffn_lut, dtype=np.float32)
    gidx = np.asarray(L0_ffn_gate_idx, dtype=np.int32)
    uidx = np.asarray(L0_ffn_up_idx, dtype=np.int32)
    didx = np.asarray(L0_ffn_down_idx, dtype=np.int32)

    waTh = np.ascontiguousarray(Wa.T)
    wxTh = np.ascontiguousarray(Wx.T)

    def fm(v):
        return np.ascontiguousarray(v.reshape(KD, 128).T)

    bafm, bxfm, lamfm = fm(ba), fm(bx), fm(Lam)
    lutin = lut.reshape(1, 16)

    gT = np.ascontiguousarray(gidx.T)
    uT = np.ascontiguousarray(uidx.T)
    dT = np.ascontiguousarray(didx.T)
    gTp = np.pad(gT, ((0, 0), (0, IPADC * 128 - I5)))
    uTp = np.pad(uT, ((0, 0), (0, IPADC * 128 - I5)))

    in_maps = []
    for c in range(8):
        seq, half = c // 2, c % 2
        xs = x[seq, half * TOK:(half + 1) * TOK, :]
        sel = np.zeros((1, 8), np.float32)
        if half == 1:
            sel[0, c - 1] = 1.0
        in_maps.append(dict(
            xin=np.ascontiguousarray(xs.T),
            waT=waTh, wxT=wxTh, bafm=bafm, bxfm=bxfm, lamfm=lamfm,
            lutin=lutin, selin=sel,
            gsh=np.ascontiguousarray(gTp[:, BCOL * c:BCOL * (c + 1)]),
            ush=np.ascontiguousarray(uTp[:, BCOL * c:BCOL * (c + 1)]),
            dsh=np.ascontiguousarray(dT[:, 256 * c:256 * (c + 1)]),
        ))

    nc = _get_nc()
    res = run_bass_kernel_spmd(nc, in_maps, core_ids=list(range(8)))

    out = np.empty((4, 4096, 2048), np.float32)
    for c in range(8):
        seq, half = c // 2, c % 2
        out[seq, half * TOK:(half + 1) * TOK, :] = np.asarray(
            res.results[c]["outfm"]).T
    return out
